# version 04
# baseline (speedup 1.0000x reference)
import numpy as np
import ml_dtypes
import concourse.bass as bass
import concourse.mybir as mybir
from concourse.bass_utils import run_bass_kernel_spmd

F32 = mybir.dt.float32
BF16 = mybir.dt.bfloat16
I32 = mybir.dt.int32
U32 = mybir.dt.uint32
AF = mybir.ActivationFunctionType
ALU = mybir.AluOpType

ENGS = ("sync", "gpsimd", "tensor", "vector", "scalar")
T = 1024
NCT = 32
NCORE = 8
EPS_LN = 1e-5
EPS_DEN = 1e-6
DTSIZE = {F32: 4, BF16: 2, I32: 4, U32: 4}


class Sched:
    def __init__(self):
        self.prog = {e: [] for e in ENGS}
        self.val = {}
        self.sem_names = []
        self.dma_sems = set()
        self.queue_sems = {"sync": set(), "gpsimd": set()}

    def sem(self, name, dma=False):
        if name not in self.val:
            self.val[name] = 0
            self.sem_names.append(name)
            if dma:
                self.dma_sems.add(name)
        return name

    def w(self, eng, sname, v):
        if v <= 0:
            return
        assert v <= self.val[sname], f"wait {sname}>={v} but scheduled {self.val[sname]}"
        if sname in self.dma_sems:
            assert v == self.val[sname], \
                f"stale wait on dma sem {sname}: {v} < {self.val[sname]}"
        self.prog[eng].append(("w", sname, v))

    def o(self, eng, fn, incs=()):
        own = {"vector": "v", "scalar": "a"}.get(eng)
        if own is not None:
            if self.val[own] > 0:
                self.prog[eng].append(("w", own, self.val[own]))
            if not any(s == own for s, _ in incs):
                incs = tuple(incs) + ((own, 1),)
        for sname, n in incs:
            self.sem(sname)
            self.val[sname] += n
        self.prog[eng].append(("o", fn, tuple(incs)))
        return {sname: self.val[sname] for sname, n in incs}

    def dma(self, q, out, in_, sname):
        self.sem(sname, dma=True)
        self.queue_sems[q].add(sname)
        r = self.o(q, lambda e, o=out, i=in_: e.dma_start(out=o, in_=i), [(sname, 16)])
        return r[sname]

    def wait_queue(self, eng, q):
        for sname in sorted(self.queue_sems[q]):
            self.w(eng, sname, self.val[sname])

    def barrier(self):
        vals = {}
        for e in ENGS:
            sname = self.sem("bar_" + e)
            if e in ("sync", "gpsimd"):
                self.wait_queue(e, e)
                self.o(e, lambda en: en.nop(), [(sname, 1)])
            else:
                own = {"vector": "v", "scalar": "a", "tensor": "t"}[e]
                self.w(e, own, self.val[own])
                self.val[sname] += 1
                self.prog[e].append(("o", lambda en: en.nop(), ((sname, 1),)))
            vals[e] = self.val[sname]
        for e in ENGS:
            for f in ENGS:
                if f != e:
                    self.w(e, "bar_" + f, vals[f])

    def emit(self, block, semobjs):
        for ename in ENGS:
            prog = self.prog[ename]

            def body(eng, prog=prog):
                for item in prog:
                    if item[0] == "w":
                        eng.wait_ge(semobjs[item[1]], item[2])
                    else:
                        inst = item[1](eng)
                        incs = item[2]
                        if incs:
                            inst.then_inc(semobjs[incs[0][0]], incs[0][1])
                            for sname, n in incs[1:]:
                                eng.nop().then_inc(semobjs[sname], n)
            getattr(block, ename)(body)


def host_prep(inp):
    perm = np.asarray(inp["L0_ffn_perm"]).astype(np.int64)
    invperm = np.argsort(perm)
    d = {}
    f32 = np.float32

    Wq = np.asarray(inp["L0_attn_Wq"])
    sel = np.concatenate([np.arange(4 * j * 128, (4 * j + 1) * 128) for j in range(8)])
    A = Wq[sel][:, perm].T
    d["wq"] = np.ascontiguousarray(A.reshape(32, 128, 2, 512).transpose(2, 1, 0, 3)).astype(f32)
    d["bq2d"] = np.ascontiguousarray(np.asarray(inp["L0_attn_bq"])[sel].reshape(8, 128).T).astype(f32)
    d["m0h"] = np.ascontiguousarray(np.asarray(inp["L0_attn_M"]).transpose(1, 0, 2)).astype(f32)
    d["z0t"] = np.ascontiguousarray(np.asarray(inp["L0_attn_z"]).T).astype(f32)
    Wo = np.asarray(inp["L0_attn_Wo"])
    d["woh"] = np.ascontiguousarray(Wo[perm, :].T.reshape(32, 128, 4096)).astype(f32)
    up0 = np.asarray(inp["L0_ffn_up"])
    d["up0h"] = np.ascontiguousarray(up0.reshape(16, 2, 128, 1024).transpose(0, 2, 1, 3)).astype(f32)
    dn0 = np.asarray(inp["L0_ffn_down"])
    d["dn0h"] = np.ascontiguousarray(dn0.reshape(16, 8, 128, 256).transpose(0, 2, 1, 3)).astype(f32)
    B = np.asarray(inp["L1_attn_up_w"])[:, perm].T
    d["upw1"] = np.ascontiguousarray(B.reshape(32, 128, 2, 512).transpose(2, 1, 0, 3)).astype(f32)
    Bg = np.asarray(inp["L1_attn_gate_w"])[:, perm].T
    d["gtw1"] = np.ascontiguousarray(Bg.reshape(32, 128, 2, 512).transpose(2, 1, 0, 3)).astype(f32)
    d["lp2d"] = np.ascontiguousarray(np.asarray(inp["L1_attn_local_point"]).reshape(8, 128).T).astype(f32)
    L = np.asarray(inp["L1_attn_local_approx_output"])[perm, :].T
    d["laoh"] = np.ascontiguousarray(L.reshape(8, 128, 8, 512).transpose(2, 1, 0, 3)).astype(f32)
    fu = np.asarray(inp["L1_attn_fuse_weight"])[perm].transpose(2, 1, 0)
    d["fuseh"] = np.ascontiguousarray(fu.reshape(6, 8, 128, 8, 512).transpose(0, 3, 2, 1, 4)).astype(f32)
    disc = np.zeros((1, 8), f32)
    disc[0, :6] = np.asarray(inp["L1_attn_discount_factor"])
    d["disc"] = disc
    G = np.asarray(inp["L1_attn_ffn_gate_w"])[:, perm].T
    d["wg1"] = np.ascontiguousarray(G.reshape(32, 128, 39, 256).transpose(2, 1, 0, 3)).astype(f32)
    U = np.asarray(inp["L1_attn_ffn_up_w"])[:, perm].T
    d["wu1"] = np.ascontiguousarray(U.reshape(32, 128, 39, 256).transpose(2, 1, 0, 3)).astype(f32)
    D = np.asarray(inp["L1_attn_ffn_down_w"])[perm, :].T
    d["wd1"] = np.ascontiguousarray(D.reshape(78, 128, 8, 512).transpose(2, 1, 0, 3)).astype(f32)
    qw = np.asarray(inp["L1_ffn_q_weights"])[:, perm]
    d["qwh"] = np.ascontiguousarray(qw.reshape(4, 128, 8, 512).transpose(2, 1, 0, 3)).astype(np.int32)

    def repl(a):
        o = a.reshape(4, 8, 4096)
        o = np.broadcast_to(o[:, :, None, :], (4, 8, 16, 4096)).reshape(4, 128, 4096)
        return np.ascontiguousarray(o.reshape(4, 128, 8, 512).transpose(2, 1, 0, 3))
    d["sbh"] = repl(np.asarray(inp["L1_ffn_scales"])[:, perm]).astype(f32)
    d["zbh"] = repl(np.asarray(inp["L1_ffn_zeros"])[:, perm]).astype(f32)
    d["bias2d"] = np.ascontiguousarray(np.asarray(inp["L1_ffn_bias"])[perm].reshape(32, 128).T).astype(f32)
    gi = np.empty((128, 32), np.uint32)
    r = np.arange(128)
    for q in range(4):
        for p in range(8):
            gi[:, q * 8 + p] = invperm[8 * (128 * q + r) + p]
    d["gidx"] = gi
    x = np.asarray(inp["L0_attn_hidden_states"])[0]
    xTp = np.ascontiguousarray(x[:, perm].T)
    shards = []
    for c in range(NCORE):
        sh = xTp[:, c * T:(c + 1) * T]
        shards.append(np.ascontiguousarray(sh.reshape(32, 128, T).transpose(1, 0, 2)).astype(f32))
    return d, shards, perm


def build_program(stage="FULL"):
    nc = bass.Bass("TRN2", target_bir_lowering=False)
    S = Sched()

    def din(name, shape, dtype):
        return nc.dram_tensor(name, list(shape), dtype, kind="ExternalInput").ap()

    xt = din("xt", (128, 32, T), F32)
    wq = din("wq", (2, 128, 32, 512), F32)
    bq2d = din("bq2d", (128, 8), F32)
    m0h = din("m0h", (128, 8, 128), F32)
    z0t = din("z0t", (128, 8), F32)
    woh = din("woh", (32, 128, 4096), F32)
    up0h = din("up0h", (16, 128, 2, 1024), F32)
    dn0h = din("dn0h", (16, 128, 8, 256), F32)
    upw1 = din("upw1", (2, 128, 32, 512), F32)
    gtw1 = din("gtw1", (2, 128, 32, 512), F32)
    lp2d = din("lp2d", (128, 8), F32)
    laoh = din("laoh", (8, 128, 8, 512), F32)
    fuseh = din("fuseh", (6, 8, 128, 8, 512), F32)
    disc = din("disc", (1, 8), F32)
    wg1 = din("wg1", (39, 128, 32, 256), F32)
    wu1 = din("wu1", (39, 128, 32, 256), F32)
    wd1 = din("wd1", (8, 128, 78, 512), F32)
    qwh = din("qwh", (8, 128, 4, 512), I32)
    sbh = din("sbh", (8, 128, 4, 512), F32)
    zbh = din("zbh", (8, 128, 4, 512), F32)
    bias2d = din("bias2d", (128, 32), F32)
    gidx = din("gidx", (128, 32), U32)

    x1sp = nc.dram_tensor("x1sp", [4096, T], F32, kind="Internal").ap()
    out2sp = nc.dram_tensor("out2sp", [128, 32, T], F32, kind="Internal").ap()
    gusp = nc.dram_tensor("gusp", [128, 78, T], BF16, kind="Internal").ap()
    bcst = nc.dram_tensor("bcst", [1, 4096], F32, kind="Internal").ap()
    out = nc.dram_tensor("out", [128, 32, T], F32, kind="ExternalOutput").ap()

    ctxmgrs = []

    def sbuf(name, shape, dtype):
        cm = nc.sbuf_tensor(name, list(shape), dtype)
        t = cm.__enter__()
        ctxmgrs.append(cm)
        return t

    A1 = sbuf("A1", [128, 16384], F32)
    A2 = sbuf("A2", [128, 16384], F32)
    A3 = sbuf("A3", [128, 16384], F32)

    def cv(arena, off_bytes, n, width, dtype=BF16):
        nb = n * width * DTSIZE[dtype]
        assert off_bytes % 4 == 0 and off_bytes + nb <= 65536, (off_bytes, nb)
        v = arena[:, off_bytes // 4: (off_bytes + nb) // 4]
        if dtype != F32:
            v = v.bitcast(dtype)
        return v.rearrange("p (a b) -> p a b", b=width)

    def rowv(arena, off_bytes, width):
        return arena[0:1, off_bytes // 4: off_bytes // 4 + width]

    ones_bf = sbuf("ones_bf", [128, 1], BF16)
    mean_b = sbuf("mean_b", [128, T], F32)
    istd_b = sbuf("istd_b", [128, T], F32)
    bq_s = sbuf("bq_s", [128, 8], F32)
    lp_s = sbuf("lp_s", [128, 8], F32)
    z0_bf = sbuf("z0_bf", [128, 8], BF16)
    m0_bf = sbuf("m0_bf", [128, 8, 128], BF16)
    bias_s = sbuf("bias_s", [128, 32], F32)
    cvec = sbuf("cvec", [128, 8], F32)
    disc_s = sbuf("disc_s", [1, 8], F32)
    gidx_s = sbuf("gidx_s", [128, 32], U32)
    eps_c = sbuf("eps_c", [1, 1], F32)
    cvrow = sbuf("cvrow", [1, 8], F32)
    mean_bf = sbuf("mean_bf", [128, T], BF16)
    istd_bf = sbuf("istd_bf", [128, T], BF16)

    ps_cm = nc.psum_tensor("ps", [128, 8, 512], F32)
    ps = ps_cm.__enter__()
    ctxmgrs.append(ps_cm)

    for s_ in ("t", "v", "a", "ms"):
        S.sem(s_)
    bank_guard = [None] * 8

    def mm(bank, lhsT, rhs, start, stop, incs=(), mrows=None):
        if start and bank_guard[bank] is not None:
            S.w("tensor", bank_guard[bank][0], bank_guard[bank][1])
            bank_guard[bank] = None
        width = rhs.shape[-1]
        dst = ps[:, bank, 0:width] if mrows is None else ps[0:mrows, bank, 0:width]
        S.o("tensor", lambda e, d=dst, l=lhsT, r=rhs, st=start, sp=stop:
            e.matmul(d, l, r, start=st, stop=sp), incs)

    def guard(banks, ev):
        for b in banks:
            bank_guard[b] = ev

    S.o("gpsimd", lambda e: e.memset(ones_bf[:], 1.0), [("ms", 1)])
    S.o("gpsimd", lambda e: e.memset(eps_c[:], float(EPS_LN)), [("ms", 1)])
    S.dma("sync", bq_s[:], bq2d, "sp")
    S.dma("sync", lp_s[:], lp2d, "sp")
    S.dma("gpsimd", z0_bf[:], z0t, "gp")
    S.dma("gpsimd", m0_bf[:], m0h, "gp")
    S.dma("sync", bias_s[:], bias2d, "sp")
    S.dma("sync", disc_s[:], disc, "sp")
    persist_s = S.dma("sync", gidx_s[:], gidx, "sp")
    persist_g = S.val["gp"]

    def stats_and_norm(src_bf, dst_bf, ready):
        sq = cv(A3, 36864, 2, T, BF16)
        rows_m = rowv(A3, 49152, 1024)
        rows_e = rowv(A3, 53248, 1024)
        rows_t = rowv(A3, 57344, 1024)
        rows_i = rowv(A3, 61440, 1024)
        t_marks = []
        S.w("tensor", "ms", 1)
        for c in range(NCT):
            if ready[c] is not None:
                S.w("scalar", ready[c][0], ready[c][1])
            if c >= 2:
                S.w("scalar", "t", t_marks[c - 2])
            S.o("scalar", lambda e, c=c: e.activation(sq[:, c % 2, :], src_bf[:, c, :], AF.Square),
                [("a", 1)])
            aval = S.val["a"]
            if ready[c] is not None:
                S.w("tensor", ready[c][0], ready[c][1])
            for th in range(2):
                mm(th, ones_bf[:], src_bf[:, c, th * 512:(th + 1) * 512],
                   start=(c == 0), stop=(c == NCT - 1), mrows=1)
            S.w("tensor", "a", aval)
            for th in range(2):
                mm(2 + th, ones_bf[:], sq[:, c % 2, th * 512:(th + 1) * 512],
                   start=(c == 0), stop=(c == NCT - 1), mrows=1,
                   incs=[("t", 1)] if th == 1 else ())
            t_marks.append(S.val["t"])
        S.w("vector", "t", t_marks[-1])
        for th in range(2):
            S.o("vector", lambda e, th=th: e.tensor_scalar(
                out=rows_m[:, th * 512:(th + 1) * 512], in0=ps[0:1, th, 0:512],
                scalar1=1.0 / 4096.0, scalar2=None, op0=ALU.mult))
            S.o("vector", lambda e, th=th: e.tensor_scalar(
                out=rows_e[:, th * 512:(th + 1) * 512], in0=ps[0:1, 2 + th, 0:512],
                scalar1=1.0 / 4096.0, scalar2=None, op0=ALU.mult))
        S.o("vector", lambda e: e.tensor_tensor(out=rows_t[:], in0=rows_m[:], in1=rows_m[:], op=ALU.mult))
        r = S.o("vector", lambda e: e.tensor_tensor(out=rows_e[:], in0=rows_e[:], in1=rows_t[:],
                                                    op=ALU.subtract), [("v", 1)])
        guard([0, 1, 2, 3], ("v", r["v"]))
        S.w("scalar", "v", r["v"])
        S.w("scalar", "ms", 2)
        r = S.o("scalar", lambda e: e.activation(rows_t[:], rows_e[:], AF.Sqrt, bias=eps_c[0:1, 0:1]),
                [("a", 1)])
        S.w("vector", "a", r["a"])
        r = S.o("vector", lambda e: e.reciprocal(rows_i[:], rows_t[:]), [("v", 1)])
        S.w("sync", "v", r["v"])
        s0 = S.dma("sync", bcst[0:1, 0:1024], rows_m[:], "sb0")
        s1 = S.dma("sync", bcst[0:1, 1024:2048], rows_i[:], "sb1")
        S.w("sync", "sb0", s0)
        S.w("sync", "sb1", s1)
        s0 = S.dma("sync", mean_b[:], bcst[0:1, 0:1024].partition_broadcast(128), "sb0")
        s1 = S.dma("sync", istd_b[:], bcst[0:1, 1024:2048].partition_broadcast(128), "sb1")
        tmpf = cv(A3, 40960, 2, T, F32)
        S.w("vector", "sb0", s0)
        S.w("vector", "sb1", s1)
        marks = []
        for c in range(NCT):
            S.o("vector", lambda e, c=c: e.tensor_tensor(
                out=tmpf[:, c % 2, :], in0=src_bf[:, c, :], in1=mean_b[:], op=ALU.subtract))
            r = S.o("vector", lambda e, c=c: e.tensor_tensor(
                out=dst_bf[:, c, :], in0=tmpf[:, c % 2, :], in1=istd_b[:], op=ALU.mult), [("v", 1)])
            marks.append(r["v"])
        return marks

    xbf = cv(A2, 0, 32, T, BF16)
    xn = xbf
    v = S.dma("gpsimd", xbf[:], xt[:], "gx")
    x_ready = [("gx", v)] * 32
    WoS = cv(A3, 0, 8, 4096, BF16)

    def wos_emit(jlist, gate_v=0):
        first = True
        for gg in range(4):
            for j in jlist:
                sname = "gwo%d" % (j % 2)
                S.sem(sname, dma=True)
                S.queue_sems["gpsimd"].add(sname)
                if first and gate_v > 0:
                    S.w("gpsimd", "v", gate_v)
                first = False
                if gg > 0:
                    S.w("gpsimd", sname, S.val[sname])
                if gg == 0:
                    wos_emit.marks[j] = S.dma("gpsimd", WoS[:, j, :], woh[4 * j + gg], sname)
                else:
                    r = S.o("gpsimd", lambda e, j=j, gg=gg: e.dma_start(
                        out=WoS[:, j, :], in_=woh[4 * j + gg], accum_op=ALU.add,
                        max_dma_last_dim=4096),
                        [(sname, 16)])
                    wos_emit.marks[j] = r[sname]
    wos_emit.marks = {}
    wos_emit([0, 1, 2, 3])
    xn_marks = stats_and_norm(xbf, xn, x_ready)
    xn_done = xn_marks[-1]

    qsel = cv(A1, 32768, 8, T, BF16)
    retr = cv(A1, 49152, 8, T, BF16)
    wq_st = [cv(A1, 0, 8, 512, BF16), cv(A1, 8192, 8, 512, BF16)]
    elu_t = cv(A1, 16384, 6, 512, F32)
    wq_ld = {}
    wq_free = {}

    def wq_issue(idx):
        qg, cg = divmod(idx, 4)
        if idx >= 2:
            S.w("gpsimd", "t", wq_free[idx - 2])
        wq_ld[idx] = S.dma("gpsimd", wq_st[idx % 2][:], wq[qg, :, cg * 8:(cg + 1) * 8, :],
                           "gq%d" % (idx % 2))
        if idx == 7:
            wos_emit([4, 5, 6, 7], gate_v=xn_done)

    wq_issue(0)
    wq_issue(1)
    for qg in range(2):
        for c in range(NCT):
            cg = c // 8
            idx = qg * 4 + cg
            if c % 8 == 0:
                S.w("tensor", "gq%d" % (idx % 2), wq_ld[idx])
            if qg == 0:
                S.w("tensor", "v", xn_marks[c])
            for o_loc in range(4):
                for th in range(2):
                    bank = o_loc * 2 + th
                    mm(bank, wq_st[idx % 2][:, c % 8, o_loc * 128:(o_loc + 1) * 128],
                       xn[:, c, th * 512:(th + 1) * 512],
                       start=(c == 0), stop=(c == NCT - 1),
                       incs=[("t", 1)] if (c % 8 == 7 and o_loc == 3 and th == 1) else ())
            if c % 8 == 7:
                wq_free[idx] = S.val["t"]
                if idx + 2 < 8:
                    wq_issue(idx + 2)
        tdone = S.val["t"]
        S.w("vector", "t", tdone)
        S.w("vector", "sp", persist_s)
        for o_loc in range(4):
            j = qg * 4 + o_loc
            for th in range(2):
                bank = o_loc * 2 + th
                sl = bank % 2
                S.o("vector", lambda e, b=bank, j=j, sl=sl: e.tensor_scalar(
                    out=elu_t[:, sl, :], in0=ps[:, b, :], scalar1=bq_s[:, j:j + 1],
                    scalar2=0.0, op0=ALU.add, op1=ALU.min), [("v", 1)])
                vv = S.val["v"]
                S.w("scalar", "v", vv)
                r = S.o("scalar", lambda e, sl=sl: e.activation(
                    elu_t[:, 2 + sl, :], elu_t[:, sl, :], AF.Exp), [("a", 1)])
                av = r["a"]
                S.o("vector", lambda e, b=bank, j=j, sl=sl: e.tensor_scalar(
                    out=elu_t[:, 4 + sl, :], in0=ps[:, b, :], scalar1=bq_s[:, j:j + 1],
                    scalar2=0.0, op0=ALU.add, op1=ALU.max))
                S.w("vector", "a", av)
                r = S.o("vector", lambda e, j=j, th=th, sl=sl: e.tensor_tensor(
                    out=qsel[:, j, th * 512:(th + 1) * 512], in0=elu_t[:, 2 + sl, :],
                    in1=elu_t[:, 4 + sl, :], op=ALU.add), [("v", 1)])
                bank_guard[bank] = ("v", r["v"])
    qsel_done = S.val["v"]

    invden = cv(A1, 24576, 1, T, F32)
    rows_d = rowv(A1, 16384, 1024)
    rows_r = rowv(A1, 20480, 1024)
    S.w("tensor", "gp", persist_g)
    S.w("tensor", "v", qsel_done)
    for j in range(8):
        nb = [0, 1] if j % 2 == 0 else [2, 3]
        db = [4, 5] if j % 2 == 0 else [6, 7]
        for th in range(2):
            mm(nb[th], m0_bf[:, j, :], qsel[:, j, th * 512:(th + 1) * 512], start=True, stop=True)
        for th in range(2):
            mm(db[th], z0_bf[:, j:j + 1], qsel[:, j, th * 512:(th + 1) * 512],
               start=True, stop=True, mrows=1,
               incs=[("t", 1)] if th == 1 else ())
        tv = S.val["t"]
        S.w("vector", "t", tv)
        for th in range(2):
            S.o("vector", lambda e, b=db[th], th=th: e.tensor_scalar(
                out=rows_d[:, th * 512:(th + 1) * 512], in0=ps[0:1, b, 0:512],
                scalar1=float(EPS_DEN), scalar2=None, op0=ALU.add))
        r = S.o("vector", lambda e: e.reciprocal(rows_r[:], rows_d[:]), [("v", 1)])
        guard(db, ("v", r["v"]))
        S.w("sync", "v", r["v"])
        sv = S.dma("sync", bcst[0:1, 2048:3072], rows_r[:], "sb0")
        S.w("sync", "sb0", sv)
        sv = S.dma("sync", invden[:, 0, :], bcst[0:1, 2048:3072].partition_broadcast(128), "sb0")
        S.w("vector", "sb0", sv)
        for th in range(2):
            r = S.o("vector", lambda e, b=nb[th], j=j, th=th: e.tensor_tensor(
                out=retr[:, j, th * 512:(th + 1) * 512], in0=ps[:, b, :],
                in1=invden[:, 0, th * 512:(th + 1) * 512], op=ALU.mult),
                [("v", 1)])
            bank_guard[nb[th]] = ("v", r["v"])
    retr_done = S.val["v"]

    up_st = [cv(A1, 0, 2, T, BF16), cv(A1, 4096, 2, T, BF16)]
    dn_st = [cv(A1, 8192, 8, 256, BF16), cv(A1, 12288, 8, 256, BF16)]
    x_st = cv(A1, 16384, 4, 512, F32)
    x1_st = cv(A1, 24576, 4, 512, F32)
    t1g = cv(A1, 32768, 8, T, BF16)
    S.w("gpsimd", "v", retr_done)
    S.w("sync", "v", retr_done)
    S.w("tensor", "gwo0", S.val["gwo0"])
    S.w("tensor", "gwo1", S.val["gwo1"])
    up_ld = {}
    up_free = {}
    x_ld = {}
    x_used = {}
    x1_store = {}

    def up_issue(h):
        if h >= 2:
            S.w("gpsimd", "t", up_free[h - 2])
        S.dma("gpsimd", up_st[h % 2][:], up0h[h], "gu%d" % (h % 2))
        up_ld[h] = S.dma("gpsimd", dn_st[h % 2][:], dn0h[h], "gu%d" % (h % 2))

    def x_issue(i):
        h, rem = divmod(i, 4)
        c_loc, th = divmod(rem, 2)
        if i >= 4:
            S.w("sync", "v", x_used[i - 4])
        x_ld[i] = S.dma("sync", x_st[:, i % 4, :],
                        xt[:, 2 * h + c_loc, th * 512:(th + 1) * 512], "sx%d" % (i % 4))

    up_issue(0)
    up_issue(1)
    for i in range(4):
        x_issue(i)
    for h in range(16):
        slot = h % 2
        S.w("tensor", "gu%d" % (h % 2), up_ld[h])
        gelu_th = {}
        for th in range(2):
            for mg in range(2):
                bankset = mg * 4
                for m_loc in range(4):
                    bank = bankset + m_loc
                    for k in range(2):
                        mm(bank, up_st[slot][:, k, mg * 512 + m_loc * 128: mg * 512 + (m_loc + 1) * 128],
                           xn[:, 2 * h + k, th * 512:(th + 1) * 512],
                           start=(k == 0), stop=(k == 1),
                           incs=[("t", 1)] if (m_loc == 3 and k == 1) else ())
                tv = S.val["t"]
                S.w("scalar", "t", tv)
                for m_loc in range(4):
                    bank = bankset + m_loc
                    r = S.o("scalar", lambda e, b=bank, mt=mg * 4 + m_loc, th=th: e.activation(
                        t1g[:, mt, th * 512:(th + 1) * 512], ps[:, b, :], AF.Gelu),
                        [("a", 1)])
                    bank_guard[bank] = ("a", r["a"])
            gelu_th[th] = S.val["a"]
        for c_loc in range(2):
            c = 2 * h + c_loc
            for th in range(2):
                i = h * 4 + c_loc * 2 + th
                bank = c_loc * 2 + th
                S.w("tensor", "a", gelu_th[th])
                for mk in range(8):
                    mm(bank, dn_st[slot][:, mk, c_loc * 128:(c_loc + 1) * 128],
                       t1g[:, mk, th * 512:(th + 1) * 512],
                       start=(mk == 0), stop=False)
                for j in range(8):
                    mm(bank, WoS[:, j, c * 128:(c + 1) * 128],
                       retr[:, j, th * 512:(th + 1) * 512],
                       start=False, stop=(j == 7),
                       incs=[("t", 1)] if j == 7 else ())
                tv = S.val["t"]
                S.w("vector", "t", tv)
                S.w("vector", "sx%d" % (i % 4), x_ld[i])
                if i >= 4:
                    S.w("vector", "so%d" % (i % 4), x1_store[i - 4])
                r = S.o("vector", lambda e, b=bank, i=i: e.tensor_tensor(
                    out=x1_st[:, i % 4, :], in0=ps[:, b, :], in1=x_st[:, i % 4, :], op=ALU.add),
                    [("v", 1)])
                bank_guard[bank] = ("v", r["v"])
                x_used[i] = r["v"]
                S.w("scalar", "v", r["v"])
                S.o("scalar", lambda e, c=c, th=th, i=i: e.copy(
                    xn[:, c, th * 512:(th + 1) * 512], x1_st[:, i % 4, :]), [("a", 1)])
                S.w("sync", "v", r["v"])
                sv = S.dma("sync", x1sp[128 * c:128 * (c + 1), th * 512:(th + 1) * 512],
                           x1_st[:, i % 4, :], "so%d" % (i % 4))
                if stage == "x1":
                    sv = S.dma("sync", out[:, c, th * 512:(th + 1) * 512], x1_st[:, i % 4, :],
                               "so%d" % (i % 4))
                x1_store[i] = sv
                if i + 4 < 64:
                    x_issue(i + 4)
        up_free[h] = S.val["t"]
        if h + 2 < 16:
            up_issue(h + 2)

    if stage == "x1":
        S.wait_queue("sync", "sync")
        S.wait_queue("gpsimd", "gpsimd")
        return nc, S, ctxmgrs

    S.barrier()
    for b in range(8):
        bank_guard[b] = None
    x1bf = xn
    xn1 = cv(A1, 0, 32, T, BF16)
    xn1_marks = stats_and_norm(x1bf, xn1, [None] * NCT)
    xn1_done = xn1_marks[-1]

    S.w("scalar", "sp", persist_s)
    r = S.o("scalar", lambda e: e.activation(cvrow[:], disc_s[:], AF.Exp, scale=-1.0),
            [("a", 1)])
    S.w("sync", "a", r["a"])
    sv = S.dma("sync", bcst[0:1, 3072:3080], cvrow[:], "sb2")
    S.w("sync", "sb2", sv)
    cvec_s = S.dma("sync", cvec[:], bcst[0:1, 3072:3080].partition_broadcast(128), "sb2")

    au = cv(A2, 0, 8, T, BF16)
    keep = cv(A2, 16384, 8, T, BF16)
    delta = cv(A3, 0, 8, T, BF16)
    w_st = [cv(A3, 16384, 8, 512, BF16), cv(A3, 24576, 8, 512, BF16)]
    ld_marks = {}
    ld_free = {}
    srcs = [(upw1, 0), (upw1, 1), (gtw1, 0), (gtw1, 1)]

    def iii_issue(i):
        blk, cg = divmod(i, 4)
        wt, sg = srcs[blk]
        if i >= 2:
            S.w("gpsimd", "t", ld_free[i - 2])
        ld_marks[i] = S.dma("gpsimd", w_st[i % 2][:], wt[sg, :, cg * 8:(cg + 1) * 8, :],
                            "gw%d" % (i % 2))

    iii_issue(0)
    iii_issue(1)
    for blk in range(4):
        which = "au" if blk < 2 else "delta"
        sg = blk % 2
        for c in range(NCT):
            i = blk * 4 + c // 8
            if c % 8 == 0:
                S.w("tensor", "gw%d" % (i % 2), ld_marks[i])
            if blk == 0:
                S.w("tensor", "v", xn1_marks[c])
            for s_loc in range(4):
                for th in range(2):
                    bank = s_loc * 2 + th
                    mm(bank, w_st[i % 2][:, c % 8, s_loc * 128:(s_loc + 1) * 128],
                       xn1[:, c, th * 512:(th + 1) * 512],
                       start=(c == 0), stop=(c == NCT - 1),
                       incs=[("t", 1)] if (c % 8 == 7 and s_loc == 3 and th == 1) else ())
            if c % 8 == 7:
                ld_free[i] = S.val["t"]
                if i + 2 < 16:
                    iii_issue(i + 2)
        tdone = S.val["t"]
        if which == "au":
            S.w("scalar", "t", tdone)
            for s_loc in range(4):
                sk = sg * 4 + s_loc
                for th in range(2):
                    bank = s_loc * 2 + th
                    r = S.o("scalar", lambda e, b=bank, sk=sk, th=th: e.copy(
                        au[:, sk, th * 512:(th + 1) * 512], ps[:, b, :]), [("a", 1)])
                    bank_guard[bank] = ("a", r["a"])
        else:
            S.w("vector", "t", tdone)
            S.w("vector", "sp", persist_s)
            for s_loc in range(4):
                sk = sg * 4 + s_loc
                for th in range(2):
                    bank = s_loc * 2 + th
                    r = S.o("vector", lambda e, b=bank, sk=sk, th=th: e.tensor_scalar(
                        out=delta[:, sk, th * 512:(th + 1) * 512], in0=ps[:, b, :],
                        scalar1=lp_s[:, sk:sk + 1], scalar2=None, op0=ALU.subtract),
                        [("v", 1)])
                    bank_guard[bank] = ("v", r["v"])
    au_done_a = S.val["a"]
    delta_done = S.val["v"]

    wg_st = [cv(A2, 32768, 32, 256, BF16), cv(A3, 32768, 32, 256, BF16)]
    wu_st = [cv(A2, 49152, 32, 256, BF16), cv(A3, 49152, 32, 256, BF16)]
    silu_t = cv(A3, 16384, 4, 512, F32)
    gu_st = [cv(A3, 24576, 2, T, BF16), cv(A3, 28672, 2, T, BF16)]
    wg_ld = {}
    wg_free = {}
    gu_sp = {}

    def v_issue(fg):
        if fg == 0:
            S.w("gpsimd", "v", delta_done)
            S.w("gpsimd", "a", au_done_a)
        if fg >= 2:
            S.w("gpsimd", "t", wg_free[fg - 2])
        S.dma("gpsimd", wg_st[fg % 2][:], wg1[fg], "gw%d" % (fg % 2))
        wg_ld[fg] = S.dma("gpsimd", wu_st[fg % 2][:], wu1[fg], "gw%d" % (fg % 2))

    v_issue(0)
    v_issue(1)
    for fg in range(39):
        slot = fg % 2
        S.w("tensor", "gw%d" % (fg % 2), wg_ld[fg])
        for c in range(NCT):
            for floc in range(2):
                for th in range(2):
                    mm(floc * 2 + th, wg_st[slot][:, c, floc * 128:(floc + 1) * 128],
                       xn1[:, c, th * 512:(th + 1) * 512], start=(c == 0), stop=(c == NCT - 1))
                    mm(4 + floc * 2 + th, wu_st[slot][:, c, floc * 128:(floc + 1) * 128],
                       xn1[:, c, th * 512:(th + 1) * 512], start=(c == 0), stop=(c == NCT - 1),
                       incs=[("t", 1)] if (c == NCT - 1 and floc == 1 and th == 1) else ())
        wg_free[fg] = S.val["t"]
        if fg + 2 < 39:
            v_issue(fg + 2)
        tv = S.val["t"]
        S.w("scalar", "t", tv)
        S.w("vector", "t", tv)
        if fg >= 2:
            S.w("vector", "sg%d" % (fg % 2), gu_sp[fg - 2])
        for floc in range(2):
            for th in range(2):
                gb = floc * 2 + th
                ub = 4 + floc * 2 + th
                ssl = floc * 2 + th
                r = S.o("scalar", lambda e, b=gb, ssl=ssl: e.activation(
                    silu_t[:, ssl, :], ps[:, b, :], AF.Silu), [("a", 1)])
                bank_guard[gb] = ("a", r["a"])
                S.w("vector", "a", r["a"])
                r = S.o("vector", lambda e, b=ub, ssl=ssl, gsl=slot, floc=floc, th=th: e.tensor_tensor(
                    out=gu_st[gsl][:, floc, th * 512:(th + 1) * 512], in0=silu_t[:, ssl, :],
                    in1=ps[:, b, :], op=ALU.mult), [("v", 1)])
                bank_guard[ub] = ("v", r["v"])
        vv = S.val["v"]
        S.w("sync", "v", vv)
        gu_sp[fg] = S.dma("sync", gusp[:, 2 * fg:2 * fg + 2, :], gu_st[slot][:],
                          "sg%d" % (fg % 2))
    gu_done = (S.val["sg0"], S.val["sg1"])
    ffn2_t = S.val["t"]

    pbuf = cv(A3, 16384, 8, T, F32)
    opnd = [au,
            cv(A1, 0, 8, T, BF16), cv(A1, 16384, 8, T, BF16),
            cv(A1, 32768, 8, T, BF16), cv(A1, 49152, 8, T, BF16),
            cv(A3, 49152, 8, T, BF16), cv(A3, 0, 8, T, BF16)]
    opnd_mark = {}
    S.w("vector", "sb2", cvec_s)
    S.w("vector", "sg0", gu_done[0])
    S.w("vector", "sg1", gu_done[1])
    S.w("vector", "t", ffn2_t)
    S.o("vector", lambda e: e.tensor_scalar(out=opnd[1][:], in0=delta[:],
                                            scalar1=cvec[:, 0:1], scalar2=None, op0=ALU.mult))
    r = S.o("vector", lambda e: e.tensor_tensor(out=opnd[1][:], in0=opnd[1][:], in1=au[:], op=ALU.mult))
    opnd_mark[1] = r["v"]
    S.o("vector", lambda e: e.tensor_tensor(out=pbuf[:], in0=delta[:], in1=delta[:], op=ALU.mult))
    S.o("vector", lambda e: e.tensor_scalar(out=keep[:], in0=pbuf[:], scalar1=float(2.5 ** 2),
                                            scalar2=None, op0=ALU.is_le))
    S.o("vector", lambda e: e.tensor_scalar(out=opnd[2][:], in0=pbuf[:],
                                            scalar1=cvec[:, 1:2], scalar2=None, op0=ALU.mult))
    r = S.o("vector", lambda e: e.tensor_tensor(out=opnd[2][:], in0=opnd[2][:], in1=au[:], op=ALU.mult))
    opnd_mark[2] = r["v"]
    for m_ in (3, 4, 5, 6):
        S.o("vector", lambda e: e.tensor_tensor(out=pbuf[:], in0=pbuf[:], in1=delta[:], op=ALU.mult))
        if m_ >= 5:
            S.o("vector", lambda e, m_=m_: e.tensor_tensor(out=opnd[m_][:], in0=pbuf[:], in1=keep[:],
                                                           op=ALU.mult))
            S.o("vector", lambda e, m_=m_: e.tensor_scalar(out=opnd[m_][:], in0=opnd[m_][:],
                                                           scalar1=cvec[:, m_ - 1:m_], scalar2=None,
                                                           op0=ALU.mult))
        else:
            S.o("vector", lambda e, m_=m_: e.tensor_scalar(out=opnd[m_][:], in0=pbuf[:],
                                                           scalar1=cvec[:, m_ - 1:m_], scalar2=None,
                                                           op0=ALU.mult))
        r = S.o("vector", lambda e, m_=m_: e.tensor_tensor(out=opnd[m_][:], in0=opnd[m_][:], in1=au[:],
                                                           op=ALU.mult), [("v", 1)])
        opnd_mark[m_] = r["v"]
    opnd_done = S.val["v"]

    r = S.o("scalar", lambda e: e.copy(mean_bf[:], mean_b[:]))
    r = S.o("scalar", lambda e: e.copy(istd_bf[:], istd_b[:]), [("a", 1)])
    stats_bf_a = r["a"]

    CH = 5
    chunks = [(i * CH, min(CH, 78 - i * CH)) for i in range((78 + CH - 1) // CH)]
    lf_st = [cv(A3, 16384, 8, 512, BF16), cv(A3, 24576, 8, 512, BF16)]
    dn2_st = [cv(A2, 32768, CH, 512, BF16), cv(A2, 37888, CH, 512, BF16)]
    gu2_st = [cv(A2, 43008, CH, T, BF16), cv(A2, 53248, CH, T, BF16)]
    o2_st = cv(A3, 32768, 2, 512, F32)
    x1r_st = cv(A3, 36864, 2, 512, F32)
    gat_bf = cv(A2, 16384, 8, T, BF16)
    lf_ld = {}
    lf_free = {}
    dn2_ld = {}
    dn2_free = {}
    o2_marks = {}
    x1r_ld = {}
    x1r_free = {}
    nch = len(chunks)

    def lf_issue(n):
        og, ti = divmod(n, 7)
        src = laoh[og] if ti == 0 else fuseh[ti - 1, og]
        if n == 0:
            S.w("gpsimd", "v", opnd_done)
        if n >= 2:
            S.w("gpsimd", "t", lf_free[n - 2])
        lf_ld[n] = S.dma("gpsimd", lf_st[n % 2][:], src, "gl%d" % (n % 2))

    def dn2_issue(n):
        og, ci = divmod(n, nch)
        f0, cnt = chunks[ci]
        if n == 0:
            S.w("gpsimd", "t", ffn2_t)
            S.w("sync", "t", ffn2_t)
            S.w("sync", "sg0", gu_done[0])
            S.w("sync", "sg1", gu_done[1])
        if n >= 2:
            S.w("gpsimd", "t", dn2_free[n - 2])
            S.w("sync", "t", dn2_free[n - 2])
        gm = S.dma("gpsimd", dn2_st[n % 2][:, 0:cnt, :], wd1[og, :, f0:f0 + cnt, :],
                   "gd%d" % (n % 2))
        sm = S.dma("sync", gu2_st[n % 2][:, 0:cnt, :], gusp[:, f0:f0 + cnt, :],
                   "sr%d" % (n % 2))
        dn2_ld[n] = (gm, sm)

    def x1r_issue(i):
        og, rem = divmod(i, 8)
        o_loc, th = divmod(rem, 2)
        c = og * 4 + o_loc
        if i == 0:
            S.w("sync", "v", opnd_done)
        if i >= 2:
            S.w("sync", "v", x1r_free[i - 2])
        x1r_ld[i] = S.dma("sync", x1r_st[:, i % 2, :],
                          x1sp[128 * c:128 * (c + 1), th * 512:(th + 1) * 512],
                          "sx%d" % (2 + i % 2))

    lf_issue(0)
    lf_issue(1)
    dn2_issue(0)
    dn2_issue(1)
    x1r_issue(0)
    x1r_issue(1)

    gat_ld = {}
    gat_norm = {}

    def gat_issue(kk):
        if kk == 0:
            S.w("gpsimd", "v", opnd_done)
            S.w("gpsimd", "sp", persist_s)
        if kk >= 8:
            S.w("gpsimd", "v", gat_norm[kk - 8])
        sname = S.sem("gg%d" % (kk % 8), dma=True)
        S.queue_sems["gpsimd"].add(sname)
        gat_ld[kk] = S.o("gpsimd", lambda e, kk=kk: e.indirect_dma_start(
            out=gat_bf[:, kk % 8, :], out_offset=None, in_=x1sp,
            in_offset=bass.IndirectOffsetOnAxis(ap=gidx_s[:, kk:kk + 1], axis=0)),
            [(sname, 16)])[sname]

    for kk in range(8):
        gat_issue(kk)

    for og in range(8):
        for ci in range(nch):
            n = og * nch + ci
            f0, cnt = chunks[ci]
            gm, sm = dn2_ld[n]
            S.w("tensor", "gd%d" % (n % 2), gm)
            S.w("tensor", "sr%d" % (n % 2), sm)
            for fi in range(cnt):
                for o_loc in range(4):
                    for th in range(2):
                        bank = o_loc * 2 + th
                        mm(bank, dn2_st[n % 2][:, fi, o_loc * 128:(o_loc + 1) * 128],
                           gu2_st[n % 2][:, fi, th * 512:(th + 1) * 512],
                           start=(ci == 0 and fi == 0), stop=False,
                           incs=[("t", 1)] if (fi == cnt - 1 and o_loc == 3 and th == 1) else ())
            dn2_free[n] = S.val["t"]
            if n + 2 < 8 * nch:
                dn2_issue(n + 2)
        for ti in range(7):
            n = og * 7 + ti
            S.w("tensor", "gl%d" % (n % 2), lf_ld[n])
            if ti >= 1:
                S.w("tensor", "v", opnd_mark[ti])
            for sk in range(8):
                for o_loc in range(4):
                    for th in range(2):
                        bank = o_loc * 2 + th
                        mm(bank, lf_st[n % 2][:, sk, o_loc * 128:(o_loc + 1) * 128],
                           opnd[ti][:, sk, th * 512:(th + 1) * 512],
                           start=False, stop=(ti == 6 and sk == 7),
                           incs=[("t", 1)] if (sk == 7 and o_loc == 3 and th == 1) else ())
            lf_free[n] = S.val["t"]
            if n + 2 < 56:
                lf_issue(n + 2)
        tv = S.val["t"]
        S.w("vector", "t", tv)
        for o_loc in range(4):
            c = og * 4 + o_loc
            for th in range(2):
                i = og * 8 + o_loc * 2 + th
                bank = o_loc * 2 + th
                S.w("vector", "sx%d" % (2 + i % 2), x1r_ld[i])
                if i >= 2:
                    S.w("vector", "sc%d" % (i % 2), o2_marks[i - 2])
                r = S.o("vector", lambda e, b=bank, osl=i % 2: e.tensor_tensor(
                    out=o2_st[:, osl, :], in0=ps[:, b, :], in1=x1r_st[:, osl, :],
                    op=ALU.add), [("v", 1)])
                bank_guard[bank] = ("v", r["v"])
                x1r_free[i] = r["v"]
                S.w("sync", "v", r["v"])
                o2_marks[i] = S.dma("sync", out2sp[:, c, th * 512:(th + 1) * 512],
                                    o2_st[:, i % 2, :], "sc%d" % (i % 2))
                if i + 2 < 64:
                    x1r_issue(i + 2)
    out2_done = (S.val["sc0"], S.val["sc1"])
    pass2_t = S.val["t"]

    qw_st = [cv(A2, 0, 4, 512, I32), cv(A2, 8192, 4, 512, I32)]
    sb_st = [cv(A2, 32768, 4, 512, F32), cv(A2, 40960, 4, 512, F32)]
    zb_st = [cv(A2, 49152, 4, 512, F32), cv(A2, 57344, 4, 512, F32)]
    nib_t = cv(A3, 57344, 2, 512, I32)
    wsl = cv(A3, 61440, 4, 512, BF16)
    o2r_st = cv(A3, 40960, 4, 512, F32)
    fin_st = cv(A3, 49152, 4, 512, F32)
    xn1g = cv(A1, 0, 32, T, BF16)
    qw_ld = {}
    qw_free = {}
    o2r_ld = {}
    fin_used = {}
    fin_store = {}

    def normalize_emit(kk):
        S.w("vector", "gg%d" % (kk % 8), gat_ld[kk])
        if kk == 0:
            S.w("vector", "t", pass2_t)
            S.w("vector", "a", stats_bf_a)
        S.o("vector", lambda e, kk=kk: e.tensor_tensor(
            out=xn1g[:, kk, :], in0=gat_bf[:, kk % 8, :], in1=mean_bf[:], op=ALU.subtract))
        r = S.o("vector", lambda e, kk=kk: e.tensor_tensor(
            out=xn1g[:, kk, :], in0=xn1g[:, kk, :], in1=istd_bf[:], op=ALU.mult), [("v", 1)])
        gat_norm[kk] = r["v"]
        if kk + 8 < 32:
            gat_issue(kk + 8)

    def qw_issue(og):
        if og == 0:
            S.w("gpsimd", "t", pass2_t)
        if og >= 2:
            S.w("gpsimd", "t", qw_free[og - 2])
        S.dma("gpsimd", qw_st[og % 2][:], qwh[og], "gw%d" % (og % 2))
        S.dma("gpsimd", sb_st[og % 2][:], sbh[og], "gw%d" % (og % 2))
        qw_ld[og] = S.dma("gpsimd", zb_st[og % 2][:], zbh[og], "gw%d" % (og % 2))

    def o2r_issue(i):
        og, rem = divmod(i, 8)
        o_loc, th = divmod(rem, 2)
        c = og * 4 + o_loc
        if i == 0:
            S.w("sync", "sc0", out2_done[0])
            S.w("sync", "sc1", out2_done[1])
            S.w("sync", "t", pass2_t)
        if i >= 4:
            S.w("sync", "v", fin_used[i - 4])
        o2r_ld[i] = S.dma("sync", o2r_st[:, i % 4, :],
                          out2sp[:, c, th * 512:(th + 1) * 512], "sx%d" % (i % 4))

    qw_issue(0)
    qw_issue(1)
    for i in range(4):
        o2r_issue(i)
    wsl_t = {}
    for og in range(8):
        S.w("vector", "gw%d" % (og % 2), qw_ld[og])
        for q in range(4):
            for p in range(8):
                kk = q * 8 + p
                if og == 0:
                    normalize_emit(kk)
                wi = kk % 4
                S.o("vector", lambda e, q=q, p=p, og=og: e.tensor_scalar(
                    out=nib_t[:, 0, :], in0=qw_st[og % 2][:, q, :],
                    scalar1=int(4 * p), scalar2=int(15),
                    op0=ALU.logical_shift_right, op1=ALU.bitwise_and))
                S.o("vector", lambda e, q=q, og=og: e.tensor_tensor(
                    out=nib_t[:, 1, :].bitcast(F32), in0=nib_t[:, 0, :],
                    in1=sb_st[og % 2][:, q, :], op=ALU.mult))
                gk = og * 32 + kk
                if gk >= 4:
                    S.w("vector", "t", wsl_t[gk - 4])
                r = S.o("vector", lambda e, q=q, og=og, wi=wi: e.tensor_tensor(
                    out=wsl[:, wi, :], in0=nib_t[:, 1, :].bitcast(F32),
                    in1=zb_st[og % 2][:, q, :], op=ALU.subtract), [("v", 1)])
                S.w("tensor", "v", r["v"])
                if og == 0:
                    S.w("tensor", "v", gat_norm[kk])
                for o_loc in range(4):
                    for th in range(2):
                        bank = o_loc * 2 + th
                        mm(bank, wsl[:, wi, o_loc * 128:(o_loc + 1) * 128],
                           xn1g[:, kk, th * 512:(th + 1) * 512],
                           start=(kk == 0), stop=(kk == 31),
                           incs=[("t", 1)] if (o_loc == 3 and th == 1) else ())
                wsl_t[gk] = S.val["t"]
        qw_free[og] = S.val["t"]
        if og + 2 < 8:
            qw_issue(og + 2)
        tv = S.val["t"]
        S.w("vector", "t", tv)
        S.w("vector", "sp", persist_s)
        for o_loc in range(4):
            c = og * 4 + o_loc
            for th in range(2):
                i = og * 8 + o_loc * 2 + th
                bank = o_loc * 2 + th
                fsl = i % 4
                S.w("vector", "sx%d" % (i % 4), o2r_ld[i])
                if i >= 4:
                    S.w("vector", "so%d" % (i % 4), fin_store[i - 4])
                r = S.o("vector", lambda e, b=bank, c=c, fsl=fsl, i=i: e.scalar_tensor_tensor(
                    out=fin_st[:, fsl, :], in0=ps[:, b, :], scalar=bias_s[:, c:c + 1],
                    in1=o2r_st[:, i % 4, :], op0=ALU.add, op1=ALU.add), [("v", 1)])
                bank_guard[bank] = ("v", r["v"])
                fin_used[i] = r["v"]
                S.w("sync", "v", r["v"])
                fin_store[i] = S.dma("sync", out[:, c, th * 512:(th + 1) * 512], fin_st[:, fsl, :],
                                     "so%d" % (i % 4))
                if i + 4 < 64:
                    o2r_issue(i + 4)
    S.wait_queue("sync", "sync")
    S.wait_queue("gpsimd", "gpsimd")
    return nc, S, ctxmgrs


def build(stage="FULL"):
    nc, S, ctxmgrs = build_program(stage)
    semobjs = {}
    sem_cms = []
    for name in S.sem_names:
        cm = nc.semaphore(name)
        semobjs[name] = cm.__enter__()
        sem_cms.append(cm)
    with nc.Block() as block:
        S.emit(block, semobjs)
    for cm in reversed(sem_cms):
        cm.__exit__(None, None, None)
    for cm in reversed(ctxmgrs):
        cm.__exit__(None, None, None)
    return nc


def kernel(**inputs):
    d, shards, perm = host_prep(inputs)
    nc = build("FULL")
    in_maps = []
    for c in range(NCORE):
        m = dict(d)
        m["xt"] = shards[c]
        in_maps.append(m)
    res = run_bass_kernel_spmd(nc, in_maps, core_ids=list(range(NCORE)))
    outp = np.empty((1, 8192, 4096), np.float32)
    for c in range(NCORE):
        sh = res.results[c]["out"]
        slotmaj = sh.transpose(1, 0, 2).reshape(4096, T)
        seg = np.empty((T, 4096), np.float32)
        seg[:, perm] = slotmaj.T
        outp[0, c * T:(c + 1) * T, :] = seg
    return outp


# version 06
# speedup vs baseline: 1.0688x; 1.0688x over previous
import numpy as np
import ml_dtypes
import concourse.bass as bass
import concourse.mybir as mybir
from concourse.bass_utils import run_bass_kernel_spmd

F32 = mybir.dt.float32
BF16 = mybir.dt.bfloat16
I32 = mybir.dt.int32
U32 = mybir.dt.uint32
AF = mybir.ActivationFunctionType
ALU = mybir.AluOpType

ENGS = ("sync", "gpsimd", "tensor", "vector", "scalar")
T = 1024
NCT = 32
NCORE = 8
EPS_LN = 1e-5
EPS_DEN = 1e-6
DTSIZE = {F32: 4, BF16: 2, I32: 4, U32: 4}


class Sched:
    def __init__(self):
        self.prog = {e: [] for e in ENGS}
        self.val = {}
        self.sem_names = []
        self.dma_sems = set()
        self.queue_sems = {"sync": set(), "gpsimd": set()}

    def sem(self, name, dma=False):
        if name not in self.val:
            self.val[name] = 0
            self.sem_names.append(name)
            if dma:
                self.dma_sems.add(name)
        return name

    def w(self, eng, sname, v):
        if v <= 0:
            return
        assert v <= self.val[sname], f"wait {sname}>={v} but scheduled {self.val[sname]}"
        if sname in self.dma_sems:
            assert v == self.val[sname], \
                f"stale wait on dma sem {sname}: {v} < {self.val[sname]}"
        self.prog[eng].append(("w", sname, v))

    def o(self, eng, fn, incs=()):
        own = {"vector": "v", "scalar": "a"}.get(eng)
        if own is not None:
            if self.val[own] > 0:
                self.prog[eng].append(("w", own, self.val[own]))
            if not any(s == own for s, _ in incs):
                incs = tuple(incs) + ((own, 1),)
        for sname, n in incs:
            self.sem(sname)
            self.val[sname] += n
        self.prog[eng].append(("o", fn, tuple(incs)))
        return {sname: self.val[sname] for sname, n in incs}

    def dma(self, q, out, in_, sname):
        self.sem(sname, dma=True)
        self.queue_sems[q].add(sname)
        r = self.o(q, lambda e, o=out, i=in_: e.dma_start(out=o, in_=i), [(sname, 16)])
        return r[sname]

    def wait_queue(self, eng, q):
        for sname in sorted(self.queue_sems[q]):
            self.w(eng, sname, self.val[sname])

    def barrier(self):
        vals = {}
        for e in ENGS:
            sname = self.sem("bar_" + e)
            if e in ("sync", "gpsimd"):
                self.wait_queue(e, e)
                self.o(e, lambda en: en.nop(), [(sname, 1)])
            else:
                own = {"vector": "v", "scalar": "a", "tensor": "t"}[e]
                self.w(e, own, self.val[own])
                self.val[sname] += 1
                self.prog[e].append(("o", lambda en: en.nop(), ((sname, 1),)))
            vals[e] = self.val[sname]
        for e in ENGS:
            for f in ENGS:
                if f != e:
                    self.w(e, "bar_" + f, vals[f])

    def emit(self, block, semobjs):
        for ename in ENGS:
            prog = self.prog[ename]

            def body(eng, prog=prog):
                for item in prog:
                    if item[0] == "w":
                        eng.wait_ge(semobjs[item[1]], item[2])
                    else:
                        inst = item[1](eng)
                        incs = item[2]
                        if incs:
                            inst.then_inc(semobjs[incs[0][0]], incs[0][1])
                            for sname, n in incs[1:]:
                                eng.nop().then_inc(semobjs[sname], n)
            getattr(block, ename)(body)


def host_prep(inp):
    perm = np.asarray(inp["L0_ffn_perm"]).astype(np.int64)
    invperm = np.argsort(perm)
    d = {}
    f32 = np.float32

    Wq = np.asarray(inp["L0_attn_Wq"])
    sel = np.concatenate([np.arange(4 * j * 128, (4 * j + 1) * 128) for j in range(8)])
    A = Wq[sel][:, perm].T
    d["wq"] = np.ascontiguousarray(A.reshape(32, 128, 2, 512).transpose(2, 1, 0, 3)).astype(f32)
    d["bq2d"] = np.ascontiguousarray(np.asarray(inp["L0_attn_bq"])[sel].reshape(8, 128).T).astype(f32)
    d["m0h"] = np.ascontiguousarray(np.asarray(inp["L0_attn_M"]).transpose(1, 0, 2)).astype(f32)
    d["z0t"] = np.ascontiguousarray(np.asarray(inp["L0_attn_z"]).T).astype(f32)
    Wo = np.asarray(inp["L0_attn_Wo"])
    d["woh"] = np.ascontiguousarray(Wo[perm, :].T.reshape(32, 128, 4096)).astype(f32)
    up0 = np.asarray(inp["L0_ffn_up"])
    d["up0h"] = np.ascontiguousarray(up0.reshape(16, 2, 128, 1024).transpose(0, 2, 1, 3)).astype(f32)
    dn0 = np.asarray(inp["L0_ffn_down"])
    d["dn0h"] = np.ascontiguousarray(dn0.reshape(16, 8, 128, 256).transpose(0, 2, 1, 3)).astype(f32)
    B = np.asarray(inp["L1_attn_up_w"])[:, perm].T
    d["upw1"] = np.ascontiguousarray(B.reshape(32, 128, 2, 512).transpose(2, 1, 0, 3)).astype(f32)
    Bg = np.asarray(inp["L1_attn_gate_w"])[:, perm].T
    d["gtw1"] = np.ascontiguousarray(Bg.reshape(32, 128, 2, 512).transpose(2, 1, 0, 3)).astype(f32)
    d["lp2d"] = np.ascontiguousarray(np.asarray(inp["L1_attn_local_point"]).reshape(8, 128).T).astype(f32)
    L = np.asarray(inp["L1_attn_local_approx_output"])[perm, :].T
    d["laoh"] = np.ascontiguousarray(L.reshape(8, 128, 8, 512).transpose(2, 1, 0, 3)).astype(f32)
    fu = np.asarray(inp["L1_attn_fuse_weight"])[perm].transpose(2, 1, 0)
    d["fuseh"] = np.ascontiguousarray(fu.reshape(6, 8, 128, 8, 512).transpose(0, 3, 2, 1, 4)).astype(f32)
    disc = np.zeros((1, 8), f32)
    disc[0, :6] = np.asarray(inp["L1_attn_discount_factor"])
    d["disc"] = disc
    G = np.asarray(inp["L1_attn_ffn_gate_w"])[:, perm].T
    d["wg1"] = np.ascontiguousarray(G.reshape(32, 128, 39, 256).transpose(2, 1, 0, 3)).astype(f32)
    U = np.asarray(inp["L1_attn_ffn_up_w"])[:, perm].T
    d["wu1"] = np.ascontiguousarray(U.reshape(32, 128, 39, 256).transpose(2, 1, 0, 3)).astype(f32)
    D = np.asarray(inp["L1_attn_ffn_down_w"])[perm, :].T
    d["wd1"] = np.ascontiguousarray(D.reshape(78, 128, 8, 512).transpose(2, 1, 0, 3)).astype(f32)
    qw = np.asarray(inp["L1_ffn_q_weights"])[:, perm]
    d["qwh"] = np.ascontiguousarray(qw.reshape(4, 128, 8, 512).transpose(2, 1, 0, 3)).astype(np.int32)

    def repl(a):
        o = a.reshape(4, 8, 4096)
        o = np.broadcast_to(o[:, :, None, :], (4, 8, 16, 4096)).reshape(4, 128, 4096)
        return np.ascontiguousarray(o.reshape(4, 128, 8, 512).transpose(2, 1, 0, 3))
    d["sbh"] = repl(np.asarray(inp["L1_ffn_scales"])[:, perm]).astype(f32)
    d["zsm"] = np.ascontiguousarray(np.asarray(inp["L1_ffn_zeros"])[:, perm]).astype(f32)
    po = np.zeros((128, 4, 32), f32)
    rr = np.arange(128)
    for q in range(4):
        po[rr, q, 8 * q + rr // 16] = 1.0
    d["poneh"] = po
    d["bias2d"] = np.ascontiguousarray(np.asarray(inp["L1_ffn_bias"])[perm].reshape(32, 128).T).astype(f32)
    gi = np.empty((128, 32), np.uint32)
    r = np.arange(128)
    for q in range(4):
        for p in range(8):
            gi[:, q * 8 + p] = invperm[8 * (128 * q + r) + p]
    d["gidx"] = gi
    x = np.asarray(inp["L0_attn_hidden_states"])[0]
    xTp = np.ascontiguousarray(x[:, perm].T)
    shards = []
    for c in range(NCORE):
        sh = xTp[:, c * T:(c + 1) * T]
        shards.append(np.ascontiguousarray(sh.reshape(32, 128, T).transpose(1, 0, 2)).astype(f32))
    return d, shards, perm


def build_program(stage="FULL"):
    nc = bass.Bass("TRN2", target_bir_lowering=False)
    S = Sched()

    def din(name, shape, dtype):
        return nc.dram_tensor(name, list(shape), dtype, kind="ExternalInput").ap()

    xt = din("xt", (128, 32, T), F32)
    wq = din("wq", (2, 128, 32, 512), F32)
    bq2d = din("bq2d", (128, 8), F32)
    m0h = din("m0h", (128, 8, 128), F32)
    z0t = din("z0t", (128, 8), F32)
    woh = din("woh", (32, 128, 4096), F32)
    up0h = din("up0h", (16, 128, 2, 1024), F32)
    dn0h = din("dn0h", (16, 128, 8, 256), F32)
    upw1 = din("upw1", (2, 128, 32, 512), F32)
    gtw1 = din("gtw1", (2, 128, 32, 512), F32)
    lp2d = din("lp2d", (128, 8), F32)
    laoh = din("laoh", (8, 128, 8, 512), F32)
    fuseh = din("fuseh", (6, 8, 128, 8, 512), F32)
    disc = din("disc", (1, 8), F32)
    wg1 = din("wg1", (39, 128, 32, 256), F32)
    wu1 = din("wu1", (39, 128, 32, 256), F32)
    wd1 = din("wd1", (8, 128, 78, 512), F32)
    qwh = din("qwh", (8, 128, 4, 512), I32)
    sbh = din("sbh", (8, 128, 4, 512), F32)
    zsm = din("zsm", (32, 4096), F32)
    poneh = din("poneh", (128, 4, 32), F32)
    bias2d = din("bias2d", (128, 32), F32)
    gidx = din("gidx", (128, 32), U32)

    x1sp = nc.dram_tensor("x1sp", [4096, T], F32, kind="Internal").ap()
    out2sp = nc.dram_tensor("out2sp", [128, 32, T], F32, kind="Internal").ap()
    gusp = nc.dram_tensor("gusp", [128, 78, T], BF16, kind="Internal").ap()
    bcst = nc.dram_tensor("bcst", [1, 4096], F32, kind="Internal").ap()
    out = nc.dram_tensor("out", [128, 32, T], F32, kind="ExternalOutput").ap()

    ctxmgrs = []

    def sbuf(name, shape, dtype):
        cm = nc.sbuf_tensor(name, list(shape), dtype)
        t = cm.__enter__()
        ctxmgrs.append(cm)
        return t

    A1 = sbuf("A1", [128, 16384], F32)
    A2 = sbuf("A2", [128, 16384], F32)
    A3 = sbuf("A3", [128, 16384], F32)

    def cv(arena, off_bytes, n, width, dtype=BF16):
        nb = n * width * DTSIZE[dtype]
        assert off_bytes % 4 == 0 and off_bytes + nb <= 65536, (off_bytes, nb)
        v = arena[:, off_bytes // 4: (off_bytes + nb) // 4]
        if dtype != F32:
            v = v.bitcast(dtype)
        return v.rearrange("p (a b) -> p a b", b=width)

    def rowv(arena, off_bytes, width):
        return arena[0:1, off_bytes // 4: off_bytes // 4 + width]

    ones_bf = sbuf("ones_bf", [128, 1], BF16)
    mean_b = sbuf("mean_b", [128, T], F32)
    istd_b = sbuf("istd_b", [128, T], F32)
    bq_s = sbuf("bq_s", [128, 8], F32)
    lp_s = sbuf("lp_s", [128, 8], F32)
    z0_bf = sbuf("z0_bf", [128, 8], BF16)
    m0_bf = sbuf("m0_bf", [128, 8, 128], BF16)
    bias_s = sbuf("bias_s", [128, 32], F32)
    cvec = sbuf("cvec", [128, 8], F32)
    disc_s = sbuf("disc_s", [1, 8], F32)
    gidx_s = sbuf("gidx_s", [128, 32], U32)
    eps_c = sbuf("eps_c", [1, 1], F32)
    cvrow = sbuf("cvrow", [1, 8], F32)
    mean_bf = sbuf("mean_bf", [128, T], BF16)
    istd_bf = sbuf("istd_bf", [128, T], BF16)
    pone_bf = sbuf("pone_bf", [128, 4, 32], BF16)

    ps_cm = nc.psum_tensor("ps", [128, 8, 512], F32)
    ps = ps_cm.__enter__()
    ctxmgrs.append(ps_cm)

    for s_ in ("t", "v", "a", "ms"):
        S.sem(s_)
    bank_guard = [None] * 8

    def mm(bank, lhsT, rhs, start, stop, incs=(), mrows=None):
        if start and bank_guard[bank] is not None:
            S.w("tensor", bank_guard[bank][0], bank_guard[bank][1])
            bank_guard[bank] = None
        width = rhs.shape[-1]
        dst = ps[:, bank, 0:width] if mrows is None else ps[0:mrows, bank, 0:width]
        S.o("tensor", lambda e, d=dst, l=lhsT, r=rhs, st=start, sp=stop:
            e.matmul(d, l, r, start=st, stop=sp), incs)

    def guard(banks, ev):
        for b in banks:
            bank_guard[b] = ev

    S.o("gpsimd", lambda e: e.memset(ones_bf[:], 1.0), [("ms", 1)])
    S.o("gpsimd", lambda e: e.memset(eps_c[:], float(EPS_LN)), [("ms", 1)])
    S.dma("sync", bq_s[:], bq2d, "sp")
    S.dma("sync", lp_s[:], lp2d, "sp")
    S.dma("gpsimd", z0_bf[:], z0t, "gp")
    S.dma("gpsimd", m0_bf[:], m0h, "gp")
    S.dma("sync", bias_s[:], bias2d, "sp")
    S.dma("sync", disc_s[:], disc, "sp")
    persist_s = S.dma("sync", gidx_s[:], gidx, "sp")
    persist_g = S.dma("gpsimd", pone_bf[:], poneh, "gp")

    def stats_and_norm(src_bf, dst_bf, ready):
        sq = cv(A3, 36864, 2, T, BF16)
        rows_m = rowv(A3, 49152, 1024)
        rows_e = rowv(A3, 53248, 1024)
        rows_t = rowv(A3, 57344, 1024)
        rows_i = rowv(A3, 61440, 1024)
        t_marks = []
        S.w("tensor", "ms", 1)
        for c in range(NCT):
            if ready[c] is not None:
                S.w("scalar", ready[c][0], ready[c][1])
            if c >= 2:
                S.w("scalar", "t", t_marks[c - 2])
            S.o("scalar", lambda e, c=c: e.activation(sq[:, c % 2, :], src_bf[:, c, :], AF.Square),
                [("a", 1)])
            aval = S.val["a"]
            if ready[c] is not None:
                S.w("tensor", ready[c][0], ready[c][1])
            for th in range(2):
                mm(th, ones_bf[:], src_bf[:, c, th * 512:(th + 1) * 512],
                   start=(c == 0), stop=(c == NCT - 1), mrows=1)
            S.w("tensor", "a", aval)
            for th in range(2):
                mm(2 + th, ones_bf[:], sq[:, c % 2, th * 512:(th + 1) * 512],
                   start=(c == 0), stop=(c == NCT - 1), mrows=1,
                   incs=[("t", 1)] if th == 1 else ())
            t_marks.append(S.val["t"])
        S.w("vector", "t", t_marks[-1])
        for th in range(2):
            S.o("vector", lambda e, th=th: e.tensor_scalar(
                out=rows_m[:, th * 512:(th + 1) * 512], in0=ps[0:1, th, 0:512],
                scalar1=1.0 / 4096.0, scalar2=None, op0=ALU.mult))
            S.o("vector", lambda e, th=th: e.tensor_scalar(
                out=rows_e[:, th * 512:(th + 1) * 512], in0=ps[0:1, 2 + th, 0:512],
                scalar1=1.0 / 4096.0, scalar2=None, op0=ALU.mult))
        S.o("vector", lambda e: e.tensor_tensor(out=rows_t[:], in0=rows_m[:], in1=rows_m[:], op=ALU.mult))
        r = S.o("vector", lambda e: e.tensor_tensor(out=rows_e[:], in0=rows_e[:], in1=rows_t[:],
                                                    op=ALU.subtract), [("v", 1)])
        guard([0, 1, 2, 3], ("v", r["v"]))
        S.w("scalar", "v", r["v"])
        S.w("scalar", "ms", 2)
        r = S.o("scalar", lambda e: e.activation(rows_t[:], rows_e[:], AF.Sqrt, bias=eps_c[0:1, 0:1]),
                [("a", 1)])
        S.w("vector", "a", r["a"])
        r = S.o("vector", lambda e: e.reciprocal(rows_i[:], rows_t[:]), [("v", 1)])
        S.w("sync", "v", r["v"])
        s0 = S.dma("sync", bcst[0:1, 0:1024], rows_m[:], "sb0")
        s1 = S.dma("sync", bcst[0:1, 1024:2048], rows_i[:], "sb1")
        S.w("sync", "sb0", s0)
        S.w("sync", "sb1", s1)
        s0 = S.dma("sync", mean_b[:], bcst[0:1, 0:1024].partition_broadcast(128), "sb0")
        s1 = S.dma("sync", istd_b[:], bcst[0:1, 1024:2048].partition_broadcast(128), "sb1")
        tmpf = cv(A3, 40960, 2, T, F32)
        S.w("vector", "sb0", s0)
        S.w("vector", "sb1", s1)
        marks = []
        for c in range(NCT):
            S.o("vector", lambda e, c=c: e.tensor_tensor(
                out=tmpf[:, c % 2, :], in0=src_bf[:, c, :], in1=mean_b[:], op=ALU.subtract))
            r = S.o("vector", lambda e, c=c: e.tensor_tensor(
                out=dst_bf[:, c, :], in0=tmpf[:, c % 2, :], in1=istd_b[:], op=ALU.mult), [("v", 1)])
            marks.append(r["v"])
        return marks

    xbf = cv(A2, 0, 32, T, BF16)
    xn = xbf
    x_ready = []
    for ch in range(4):
        v = S.dma("gpsimd", xbf[:, ch * 8:(ch + 1) * 8, :], xt[:, ch * 8:(ch + 1) * 8, :],
                  "gx%d" % ch)
        x_ready += [("gx%d" % ch, v)] * 8
    WoS = cv(A3, 0, 8, 4096, BF16)

    def wos_emit(jlist, gate_v=0):
        first = True
        for gg in range(4):
            for j in jlist:
                sname = "gwo%d" % (j % 2)
                S.sem(sname, dma=True)
                S.queue_sems["gpsimd"].add(sname)
                if first and gate_v > 0:
                    S.w("gpsimd", "v", gate_v)
                first = False
                if gg > 0:
                    S.w("gpsimd", sname, S.val[sname])
                if gg == 0:
                    wos_emit.marks[j] = S.dma("gpsimd", WoS[:, j, :], woh[4 * j + gg], sname)
                else:
                    r = S.o("gpsimd", lambda e, j=j, gg=gg: e.dma_start(
                        out=WoS[:, j, :], in_=woh[4 * j + gg], accum_op=ALU.add,
                        max_dma_last_dim=4096),
                        [(sname, 16)])
                    wos_emit.marks[j] = r[sname]
    wos_emit.marks = {}
    wq_st = [cv(A1, 0, 8, 512, BF16), cv(A1, 8192, 8, 512, BF16)]
    wq_ld = {}
    wq_free = {}

    def wq_issue(idx):
        qg, cg = divmod(idx, 4)
        if idx >= 2:
            S.w("gpsimd", "t", wq_free[idx - 2])
        wq_ld[idx] = S.dma("gpsimd", wq_st[idx % 2][:], wq[qg, :, cg * 8:(cg + 1) * 8, :],
                           "gq%d" % (idx % 2))
        if idx == 7:
            wos_emit([4, 5, 6, 7], gate_v=xn_done)

    wq_issue(0)
    wq_issue(1)
    wos_emit([0, 1, 2, 3])
    xn_marks = stats_and_norm(xbf, xn, x_ready)
    xn_done = xn_marks[-1]

    qsel = cv(A1, 32768, 8, T, BF16)
    retr = cv(A1, 49152, 8, T, BF16)
    elu_t = cv(A1, 16384, 6, 512, F32)
    for qg in range(2):
        for c in range(NCT):
            cg = c // 8
            idx = qg * 4 + cg
            if c % 8 == 0:
                S.w("tensor", "gq%d" % (idx % 2), wq_ld[idx])
            if qg == 0:
                S.w("tensor", "v", xn_marks[c])
            for o_loc in range(4):
                for th in range(2):
                    bank = o_loc * 2 + th
                    mm(bank, wq_st[idx % 2][:, c % 8, o_loc * 128:(o_loc + 1) * 128],
                       xn[:, c, th * 512:(th + 1) * 512],
                       start=(c == 0), stop=(c == NCT - 1),
                       incs=[("t", 1)] if (c % 8 == 7 and o_loc == 3 and th == 1) else ())
            if c % 8 == 7:
                wq_free[idx] = S.val["t"]
                if idx + 2 < 8:
                    wq_issue(idx + 2)
        tdone = S.val["t"]
        S.w("vector", "t", tdone)
        S.w("vector", "sp", persist_s)
        for o_loc in range(4):
            j = qg * 4 + o_loc
            for th in range(2):
                bank = o_loc * 2 + th
                sl = bank % 2
                S.o("vector", lambda e, b=bank, j=j, sl=sl: e.tensor_scalar(
                    out=elu_t[:, sl, :], in0=ps[:, b, :], scalar1=bq_s[:, j:j + 1],
                    scalar2=0.0, op0=ALU.add, op1=ALU.min), [("v", 1)])
                vv = S.val["v"]
                S.w("scalar", "v", vv)
                r = S.o("scalar", lambda e, sl=sl: e.activation(
                    elu_t[:, 2 + sl, :], elu_t[:, sl, :], AF.Exp), [("a", 1)])
                av = r["a"]
                S.o("vector", lambda e, b=bank, j=j, sl=sl: e.tensor_scalar(
                    out=elu_t[:, 4 + sl, :], in0=ps[:, b, :], scalar1=bq_s[:, j:j + 1],
                    scalar2=0.0, op0=ALU.add, op1=ALU.max))
                S.w("vector", "a", av)
                r = S.o("vector", lambda e, j=j, th=th, sl=sl: e.tensor_tensor(
                    out=qsel[:, j, th * 512:(th + 1) * 512], in0=elu_t[:, 2 + sl, :],
                    in1=elu_t[:, 4 + sl, :], op=ALU.add), [("v", 1)])
                bank_guard[bank] = ("v", r["v"])
    qsel_done = S.val["v"]

    invden = [cv(A1, 8192, 1, T, F32), cv(A1, 12288, 1, T, F32)]
    rows_d = [rowv(A1, 16384, 1024), rowv(A1, 20480, 1024)]
    rows_r = [rowv(A1, 0, 1024), rowv(A1, 4096, 1024)]
    S.w("tensor", "gp", persist_g)
    S.w("tensor", "v", qsel_done)
    mm_t = {}

    def retr_mm(j):
        par = j % 2
        nb = [0, 1] if par == 0 else [2, 3]
        db = [4, 5] if par == 0 else [6, 7]
        for th in range(2):
            mm(nb[th], m0_bf[:, j, :], qsel[:, j, th * 512:(th + 1) * 512], start=True, stop=True)
        for th in range(2):
            mm(db[th], z0_bf[:, j:j + 1], qsel[:, j, th * 512:(th + 1) * 512],
               start=True, stop=True, mrows=1,
               incs=[("t", 1)] if th == 1 else ())
        mm_t[j] = S.val["t"]

    retr_mm(0)
    retr_mm(1)
    for j in range(8):
        par = j % 2
        nb = [0, 1] if par == 0 else [2, 3]
        db = [4, 5] if par == 0 else [6, 7]
        S.w("vector", "t", mm_t[j])
        for th in range(2):
            S.o("vector", lambda e, b=db[th], th=th, par=par: e.tensor_scalar(
                out=rows_d[par][:, th * 512:(th + 1) * 512], in0=ps[0:1, b, 0:512],
                scalar1=float(EPS_DEN), scalar2=None, op0=ALU.add))
        r = S.o("vector", lambda e, par=par: e.reciprocal(rows_r[par][:], rows_d[par][:]),
                [("v", 1)])
        guard(db, ("v", r["v"]))
        S.w("sync", "v", r["v"])
        sv = S.dma("sync", bcst[0:1, 2048 + 1024 * par:3072 + 1024 * par], rows_r[par][:],
                   "sb%d" % par)
        S.w("sync", "sb%d" % par, sv)
        sv = S.dma("sync", invden[par][:, 0, :],
                   bcst[0:1, 2048 + 1024 * par:3072 + 1024 * par].partition_broadcast(128),
                   "sb%d" % par)
        S.w("vector", "sb%d" % par, sv)
        for th in range(2):
            r = S.o("vector", lambda e, b=nb[th], j=j, th=th, par=par: e.tensor_tensor(
                out=retr[:, j, th * 512:(th + 1) * 512], in0=ps[:, b, :],
                in1=invden[par][:, 0, th * 512:(th + 1) * 512], op=ALU.mult),
                [("v", 1)])
            bank_guard[nb[th]] = ("v", r["v"])
        if j + 2 < 8:
            retr_mm(j + 2)
    retr_done = S.val["v"]

    up_st = [cv(A1, 0, 2, T, BF16), cv(A1, 4096, 2, T, BF16)]
    dn_st = [cv(A1, 8192, 8, 256, BF16), cv(A1, 12288, 8, 256, BF16)]
    x_st = cv(A1, 16384, 4, 512, F32)
    x1_st = cv(A1, 24576, 4, 512, F32)
    t1g = cv(A1, 32768, 8, T, BF16)
    S.w("gpsimd", "v", retr_done)
    S.w("sync", "v", retr_done)
    S.w("tensor", "gwo0", S.val["gwo0"])
    S.w("tensor", "gwo1", S.val["gwo1"])
    up_ld = {}
    up_free = {}
    x_ld = {}
    x_used = {}
    x1_store = {}

    def up_issue(h):
        if h >= 2:
            S.w("gpsimd", "t", up_free[h - 2])
        S.dma("gpsimd", up_st[h % 2][:], up0h[h], "gu%d" % (h % 2))
        up_ld[h] = S.dma("gpsimd", dn_st[h % 2][:], dn0h[h], "gu%d" % (h % 2))

    def x_issue(i):
        h, rem = divmod(i, 4)
        c_loc, th = divmod(rem, 2)
        if i >= 4:
            S.w("sync", "v", x_used[i - 4])
        x_ld[i] = S.dma("sync", x_st[:, i % 4, :],
                        xt[:, 2 * h + c_loc, th * 512:(th + 1) * 512], "sx%d" % (i % 4))

    up_issue(0)
    up_issue(1)
    for i in range(4):
        x_issue(i)
    for h in range(16):
        slot = h % 2
        S.w("tensor", "gu%d" % (h % 2), up_ld[h])
        gelu_th = {}
        for th in range(2):
            for mg in (1, 0):
                bankset = mg * 4
                for m_loc in range(4):
                    bank = bankset + m_loc
                    for k in range(2):
                        mm(bank, up_st[slot][:, k, mg * 512 + m_loc * 128: mg * 512 + (m_loc + 1) * 128],
                           xn[:, 2 * h + k, th * 512:(th + 1) * 512],
                           start=(k == 0), stop=(k == 1),
                           incs=[("t", 1)] if (m_loc == 3 and k == 1) else ())
                tv = S.val["t"]
                S.w("scalar", "t", tv)
                for m_loc in range(4):
                    bank = bankset + m_loc
                    r = S.o("scalar", lambda e, b=bank, mt=mg * 4 + m_loc, th=th: e.activation(
                        t1g[:, mt, th * 512:(th + 1) * 512], ps[:, b, :], AF.Gelu),
                        [("a", 1)])
                    bank_guard[bank] = ("a", r["a"])
            gelu_th[th] = S.val["a"]
        for c_loc in range(2):
            c = 2 * h + c_loc
            for th in range(2):
                i = h * 4 + c_loc * 2 + th
                bank = c_loc * 2 + th
                S.w("tensor", "a", gelu_th[th])
                for mk in range(8):
                    mm(bank, dn_st[slot][:, mk, c_loc * 128:(c_loc + 1) * 128],
                       t1g[:, mk, th * 512:(th + 1) * 512],
                       start=(mk == 0), stop=False)
                for j in range(8):
                    mm(bank, WoS[:, j, c * 128:(c + 1) * 128],
                       retr[:, j, th * 512:(th + 1) * 512],
                       start=False, stop=(j == 7),
                       incs=[("t", 1)] if j == 7 else ())
                tv = S.val["t"]
                S.w("vector", "t", tv)
                S.w("vector", "sx%d" % (i % 4), x_ld[i])
                if i >= 4:
                    S.w("vector", "so%d" % (i % 4), x1_store[i - 4])
                r = S.o("vector", lambda e, b=bank, i=i: e.tensor_tensor(
                    out=x1_st[:, i % 4, :], in0=ps[:, b, :], in1=x_st[:, i % 4, :], op=ALU.add),
                    [("v", 1)])
                bank_guard[bank] = ("v", r["v"])
                x_used[i] = r["v"]
                S.w("scalar", "v", r["v"])
                S.o("scalar", lambda e, c=c, th=th, i=i: e.copy(
                    xn[:, c, th * 512:(th + 1) * 512], x1_st[:, i % 4, :]), [("a", 1)])
                S.w("sync", "v", r["v"])
                sv = S.dma("sync", x1sp[128 * c:128 * (c + 1), th * 512:(th + 1) * 512],
                           x1_st[:, i % 4, :], "so%d" % (i % 4))
                if stage == "x1":
                    sv = S.dma("sync", out[:, c, th * 512:(th + 1) * 512], x1_st[:, i % 4, :],
                               "so%d" % (i % 4))
                x1_store[i] = sv
                if i + 4 < 64:
                    x_issue(i + 4)
        up_free[h] = S.val["t"]
        if h + 2 < 16:
            up_issue(h + 2)

    if stage == "x1":
        S.wait_queue("sync", "sync")
        S.wait_queue("gpsimd", "gpsimd")
        return nc, S, ctxmgrs

    S.barrier()
    for b in range(8):
        bank_guard[b] = None
    x1bf = xn
    xn1 = cv(A1, 0, 32, T, BF16)
    xn1_marks = stats_and_norm(x1bf, xn1, [None] * NCT)
    xn1_done = xn1_marks[-1]

    S.w("scalar", "sp", persist_s)
    r = S.o("scalar", lambda e: e.activation(cvrow[:], disc_s[:], AF.Exp, scale=-1.0),
            [("a", 1)])
    S.w("sync", "a", r["a"])
    sv = S.dma("sync", bcst[0:1, 3072:3080], cvrow[:], "sb2")
    S.w("sync", "sb2", sv)
    cvec_s = S.dma("sync", cvec[:], bcst[0:1, 3072:3080].partition_broadcast(128), "sb2")

    au = cv(A2, 0, 8, T, BF16)
    keep = cv(A2, 16384, 8, T, BF16)
    delta = cv(A3, 0, 8, T, BF16)
    w_st = [cv(A3, 16384, 8, 512, BF16), cv(A3, 24576, 8, 512, BF16)]
    ld_marks = {}
    ld_free = {}
    srcs = [(upw1, 0), (upw1, 1), (gtw1, 0), (gtw1, 1)]

    def iii_issue(i):
        blk, cg = divmod(i, 4)
        wt, sg = srcs[blk]
        if i >= 2:
            S.w("gpsimd", "t", ld_free[i - 2])
        ld_marks[i] = S.dma("gpsimd", w_st[i % 2][:], wt[sg, :, cg * 8:(cg + 1) * 8, :],
                            "gw%d" % (i % 2))

    iii_issue(0)
    iii_issue(1)
    for blk in range(4):
        which = "au" if blk < 2 else "delta"
        sg = blk % 2
        for c in range(NCT):
            i = blk * 4 + c // 8
            if c % 8 == 0:
                S.w("tensor", "gw%d" % (i % 2), ld_marks[i])
            if blk == 0:
                S.w("tensor", "v", xn1_marks[c])
            for s_loc in range(4):
                for th in range(2):
                    bank = s_loc * 2 + th
                    mm(bank, w_st[i % 2][:, c % 8, s_loc * 128:(s_loc + 1) * 128],
                       xn1[:, c, th * 512:(th + 1) * 512],
                       start=(c == 0), stop=(c == NCT - 1),
                       incs=[("t", 1)] if (c % 8 == 7 and s_loc == 3 and th == 1) else ())
            if c % 8 == 7:
                ld_free[i] = S.val["t"]
                if i + 2 < 16:
                    iii_issue(i + 2)
        tdone = S.val["t"]
        if which == "au":
            S.w("scalar", "t", tdone)
            for s_loc in range(4):
                sk = sg * 4 + s_loc
                for th in range(2):
                    bank = s_loc * 2 + th
                    r = S.o("scalar", lambda e, b=bank, sk=sk, th=th: e.copy(
                        au[:, sk, th * 512:(th + 1) * 512], ps[:, b, :]), [("a", 1)])
                    bank_guard[bank] = ("a", r["a"])
        else:
            S.w("vector", "t", tdone)
            S.w("vector", "sp", persist_s)
            for s_loc in range(4):
                sk = sg * 4 + s_loc
                for th in range(2):
                    bank = s_loc * 2 + th
                    r = S.o("vector", lambda e, b=bank, sk=sk, th=th: e.tensor_scalar(
                        out=delta[:, sk, th * 512:(th + 1) * 512], in0=ps[:, b, :],
                        scalar1=lp_s[:, sk:sk + 1], scalar2=None, op0=ALU.subtract),
                        [("v", 1)])
                    bank_guard[bank] = ("v", r["v"])
    au_done_a = S.val["a"]
    delta_done = S.val["v"]

    wg_st = [cv(A2, 32768, 32, 256, BF16), cv(A3, 32768, 32, 256, BF16)]
    wu_st = [cv(A2, 49152, 32, 256, BF16), cv(A3, 49152, 32, 256, BF16)]
    silu_t = cv(A3, 16384, 4, 512, F32)
    gu_st = [cv(A3, 24576, 2, T, BF16), cv(A3, 28672, 2, T, BF16)]
    wg_ld = {}
    wg_free = {}
    gu_sp = {}

    def v_issue(fg):
        if fg == 0:
            S.w("gpsimd", "v", delta_done)
            S.w("gpsimd", "a", au_done_a)
        if fg >= 2:
            S.w("gpsimd", "t", wg_free[fg - 2])
        S.dma("gpsimd", wg_st[fg % 2][:], wg1[fg], "gw%d" % (fg % 2))
        wg_ld[fg] = S.dma("gpsimd", wu_st[fg % 2][:], wu1[fg], "gw%d" % (fg % 2))

    v_issue(0)
    v_issue(1)
    for fg in range(39):
        slot = fg % 2
        S.w("tensor", "gw%d" % (fg % 2), wg_ld[fg])
        for c in range(NCT):
            for floc in range(2):
                for th in range(2):
                    mm(floc * 2 + th, wg_st[slot][:, c, floc * 128:(floc + 1) * 128],
                       xn1[:, c, th * 512:(th + 1) * 512], start=(c == 0), stop=(c == NCT - 1))
                    mm(4 + floc * 2 + th, wu_st[slot][:, c, floc * 128:(floc + 1) * 128],
                       xn1[:, c, th * 512:(th + 1) * 512], start=(c == 0), stop=(c == NCT - 1),
                       incs=[("t", 1)] if (c == NCT - 1 and floc == 1 and th == 1) else ())
        wg_free[fg] = S.val["t"]
        if fg + 2 < 39:
            v_issue(fg + 2)
        tv = S.val["t"]
        S.w("scalar", "t", tv)
        S.w("vector", "t", tv)
        if fg >= 2:
            S.w("vector", "sg%d" % (fg % 2), gu_sp[fg - 2])
        for floc in range(2):
            for th in range(2):
                gb = floc * 2 + th
                ub = 4 + floc * 2 + th
                ssl = floc * 2 + th
                r = S.o("scalar", lambda e, b=gb, ssl=ssl: e.activation(
                    silu_t[:, ssl, :], ps[:, b, :], AF.Silu), [("a", 1)])
                bank_guard[gb] = ("a", r["a"])
                S.w("vector", "a", r["a"])
                r = S.o("vector", lambda e, b=ub, ssl=ssl, gsl=slot, floc=floc, th=th: e.tensor_tensor(
                    out=gu_st[gsl][:, floc, th * 512:(th + 1) * 512], in0=silu_t[:, ssl, :],
                    in1=ps[:, b, :], op=ALU.mult), [("v", 1)])
                bank_guard[ub] = ("v", r["v"])
        vv = S.val["v"]
        S.w("sync", "v", vv)
        gu_sp[fg] = S.dma("sync", gusp[:, 2 * fg:2 * fg + 2, :], gu_st[slot][:],
                          "sg%d" % (fg % 2))
    gu_done = (S.val["sg0"], S.val["sg1"])
    ffn2_t = S.val["t"]

    pbuf = cv(A3, 16384, 8, T, BF16)
    opnd = [au,
            cv(A1, 0, 8, T, BF16), cv(A1, 16384, 8, T, BF16),
            cv(A1, 32768, 8, T, BF16), cv(A1, 49152, 8, T, BF16),
            cv(A3, 49152, 8, T, BF16), cv(A3, 0, 8, T, BF16)]
    opnd_mark = {}
    S.w("vector", "sb2", cvec_s)
    S.w("vector", "sg0", gu_done[0])
    S.w("vector", "sg1", gu_done[1])
    S.w("vector", "t", ffn2_t)
    S.o("vector", lambda e: e.tensor_scalar(out=opnd[1][:], in0=delta[:],
                                            scalar1=cvec[:, 0:1], scalar2=None, op0=ALU.mult))
    r = S.o("vector", lambda e: e.tensor_tensor(out=opnd[1][:], in0=opnd[1][:], in1=au[:], op=ALU.mult))
    opnd_mark[1] = r["v"]
    S.o("vector", lambda e: e.tensor_tensor(out=pbuf[:], in0=delta[:], in1=delta[:], op=ALU.mult))
    S.o("vector", lambda e: e.tensor_scalar(out=keep[:], in0=pbuf[:], scalar1=float(2.5 ** 2),
                                            scalar2=None, op0=ALU.is_le))
    S.o("vector", lambda e: e.tensor_scalar(out=opnd[2][:], in0=pbuf[:],
                                            scalar1=cvec[:, 1:2], scalar2=None, op0=ALU.mult))
    r = S.o("vector", lambda e: e.tensor_tensor(out=opnd[2][:], in0=opnd[2][:], in1=au[:], op=ALU.mult))
    opnd_mark[2] = r["v"]
    for m_ in (3, 4, 5, 6):
        S.o("vector", lambda e: e.tensor_tensor(out=pbuf[:], in0=pbuf[:], in1=delta[:], op=ALU.mult))
        if m_ >= 5:
            S.o("vector", lambda e, m_=m_: e.tensor_tensor(out=opnd[m_][:], in0=pbuf[:], in1=keep[:],
                                                           op=ALU.mult))
            S.o("vector", lambda e, m_=m_: e.tensor_scalar(out=opnd[m_][:], in0=opnd[m_][:],
                                                           scalar1=cvec[:, m_ - 1:m_], scalar2=None,
                                                           op0=ALU.mult))
        else:
            S.o("vector", lambda e, m_=m_: e.tensor_scalar(out=opnd[m_][:], in0=pbuf[:],
                                                           scalar1=cvec[:, m_ - 1:m_], scalar2=None,
                                                           op0=ALU.mult))
        r = S.o("vector", lambda e, m_=m_: e.tensor_tensor(out=opnd[m_][:], in0=opnd[m_][:], in1=au[:],
                                                           op=ALU.mult), [("v", 1)])
        opnd_mark[m_] = r["v"]
    opnd_done = S.val["v"]

    r = S.o("scalar", lambda e: e.copy(mean_bf[:], mean_b[:]))
    r = S.o("scalar", lambda e: e.copy(istd_bf[:], istd_b[:]), [("a", 1)])
    stats_bf_a = r["a"]

    CH = 5
    chunks = [(i * CH, min(CH, 78 - i * CH)) for i in range((78 + CH - 1) // CH)]
    lf_st = [cv(A3, 16384, 8, 512, BF16), cv(A3, 24576, 8, 512, BF16)]
    dn2_st = [cv(A2, 32768, CH, 512, BF16), cv(A2, 37888, CH, 512, BF16)]
    gu2_st = [cv(A2, 43008, CH, T, BF16), cv(A2, 53248, CH, T, BF16)]
    o2_st = cv(A3, 32768, 2, 512, F32)
    x1r_st = cv(A3, 36864, 2, 512, F32)
    gat_bf = cv(A2, 16384, 8, T, BF16)
    lf_ld = {}
    lf_free = {}
    dn2_ld = {}
    dn2_free = {}
    o2_marks = {}
    x1r_ld = {}
    x1r_free = {}
    nch = len(chunks)

    def lf_issue(n):
        og, ti = divmod(n, 7)
        src = laoh[og] if ti == 0 else fuseh[ti - 1, og]
        if n == 0:
            S.w("gpsimd", "v", opnd_done)
        if n >= 2:
            S.w("gpsimd", "t", lf_free[n - 2])
        lf_ld[n] = S.dma("gpsimd", lf_st[n % 2][:], src, "gl%d" % (n % 2))

    def dn2_issue(n):
        og, ci = divmod(n, nch)
        f0, cnt = chunks[ci]
        if n == 0:
            S.w("gpsimd", "t", ffn2_t)
            S.w("sync", "t", ffn2_t)
            S.w("sync", "sg0", gu_done[0])
            S.w("sync", "sg1", gu_done[1])
        if n >= 2:
            S.w("gpsimd", "t", dn2_free[n - 2])
            S.w("sync", "t", dn2_free[n - 2])
        gm = S.dma("gpsimd", dn2_st[n % 2][:, 0:cnt, :], wd1[og, :, f0:f0 + cnt, :],
                   "gd%d" % (n % 2))
        sm = S.dma("sync", gu2_st[n % 2][:, 0:cnt, :], gusp[:, f0:f0 + cnt, :],
                   "sr%d" % (n % 2))
        dn2_ld[n] = (gm, sm)

    def x1r_issue(i):
        og, rem = divmod(i, 8)
        o_loc, th = divmod(rem, 2)
        c = og * 4 + o_loc
        if i == 0:
            S.w("sync", "t", ffn2_t)
        if i >= 2:
            S.w("sync", "v", x1r_free[i - 2])
        x1r_ld[i] = S.dma("sync", x1r_st[:, i % 2, :],
                          x1sp[128 * c:128 * (c + 1), th * 512:(th + 1) * 512],
                          "sx%d" % (2 + i % 2))

    lf_issue(0)
    lf_issue(1)
    dn2_issue(0)
    dn2_issue(1)
    x1r_issue(0)
    x1r_issue(1)

    gat_ld = {}
    gat_norm = {}

    def gat_issue(kk):
        if kk == 0:
            S.w("gpsimd", "v", opnd_done)
            S.w("gpsimd", "sp", persist_s)
        if kk >= 8:
            S.w("gpsimd", "v", gat_norm[kk - 8])
        sname = S.sem("gg%d" % (kk % 8), dma=True)
        S.queue_sems["gpsimd"].add(sname)
        gat_ld[kk] = S.o("gpsimd", lambda e, kk=kk: e.indirect_dma_start(
            out=gat_bf[:, kk % 8, :], out_offset=None, in_=x1sp,
            in_offset=bass.IndirectOffsetOnAxis(ap=gidx_s[:, kk:kk + 1], axis=0)),
            [(sname, 16)])[sname]

    for og in range(8):
        if og == 1:
            for kk in range(8):
                gat_issue(kk)
        for ci in range(nch):
            n = og * nch + ci
            f0, cnt = chunks[ci]
            gm, sm = dn2_ld[n]
            S.w("tensor", "gd%d" % (n % 2), gm)
            S.w("tensor", "sr%d" % (n % 2), sm)
            for fi in range(cnt):
                for o_loc in range(4):
                    for th in range(2):
                        bank = o_loc * 2 + th
                        mm(bank, dn2_st[n % 2][:, fi, o_loc * 128:(o_loc + 1) * 128],
                           gu2_st[n % 2][:, fi, th * 512:(th + 1) * 512],
                           start=(ci == 0 and fi == 0), stop=False,
                           incs=[("t", 1)] if (fi == cnt - 1 and o_loc == 3 and th == 1) else ())
            dn2_free[n] = S.val["t"]
            if n + 2 < 8 * nch:
                dn2_issue(n + 2)
        for ti in range(7):
            n = og * 7 + ti
            S.w("tensor", "gl%d" % (n % 2), lf_ld[n])
            if ti >= 1:
                S.w("tensor", "v", opnd_mark[ti])
            for sk in range(8):
                for o_loc in range(4):
                    for th in range(2):
                        bank = o_loc * 2 + th
                        mm(bank, lf_st[n % 2][:, sk, o_loc * 128:(o_loc + 1) * 128],
                           opnd[ti][:, sk, th * 512:(th + 1) * 512],
                           start=False, stop=(ti == 6 and sk == 7),
                           incs=[("t", 1)] if (sk == 7 and o_loc == 3 and th == 1) else ())
            lf_free[n] = S.val["t"]
            if n + 2 < 56:
                lf_issue(n + 2)
        tv = S.val["t"]
        S.w("vector", "t", tv)
        for o_loc in range(4):
            c = og * 4 + o_loc
            for th in range(2):
                i = og * 8 + o_loc * 2 + th
                bank = o_loc * 2 + th
                S.w("vector", "sx%d" % (2 + i % 2), x1r_ld[i])
                if i >= 2:
                    S.w("vector", "sc%d" % (i % 2), o2_marks[i - 2])
                r = S.o("vector", lambda e, b=bank, osl=i % 2: e.tensor_tensor(
                    out=o2_st[:, osl, :], in0=ps[:, b, :], in1=x1r_st[:, osl, :],
                    op=ALU.add), [("v", 1)])
                bank_guard[bank] = ("v", r["v"])
                x1r_free[i] = r["v"]
                S.w("sync", "v", r["v"])
                o2_marks[i] = S.dma("sync", out2sp[:, c, th * 512:(th + 1) * 512],
                                    o2_st[:, i % 2, :], "sc%d" % (i % 2))
                if i + 2 < 64:
                    x1r_issue(i + 2)
    out2_done = (S.val["sc0"], S.val["sc1"])
    pass2_t = S.val["t"]

    qw_st = [cv(A2, 0, 4, 512, I32), cv(A2, 8192, 4, 512, I32)]
    sb_st = [cv(A2, 32768, 4, 512, F32), cv(A2, 40960, 4, 512, F32)]
    nib_t = cv(A3, 57344, 2, 512, I32)
    wsl = cv(A3, 61440, 4, 512, BF16)
    o2r_st = cv(A3, 40960, 4, 512, F32)
    fin_st = cv(A3, 49152, 4, 512, F32)
    xn1g = cv(A1, 0, 32, T, BF16)
    zsm_sb = A3[0:32, 0:4096]
    negz_bf = A3[0:32, 4096:6144].bitcast(BF16)
    colsum_bf = A3[0:32, 6144:6656].bitcast(BF16)
    qw_ld = {}
    qw_free = {}
    o2r_ld = {}
    fin_used = {}
    fin_store = {}

    def normalize_emit(kk):
        S.w("vector", "gg%d" % (kk % 8), gat_ld[kk])
        if kk == 0:
            S.w("vector", "t", pass2_t)
            S.w("vector", "a", stats_bf_a)
        S.o("vector", lambda e, kk=kk: e.tensor_tensor(
            out=xn1g[:, kk, :], in0=gat_bf[:, kk % 8, :], in1=mean_bf[:], op=ALU.subtract))
        r = S.o("vector", lambda e, kk=kk: e.tensor_tensor(
            out=xn1g[:, kk, :], in0=xn1g[:, kk, :], in1=istd_bf[:], op=ALU.mult), [("v", 1)])
        gat_norm[kk] = r["v"]
        if kk + 8 < 32:
            gat_issue(kk + 8)

    def qw_issue(og):
        if og == 0:
            S.w("gpsimd", "t", pass2_t)
        if og >= 2:
            S.w("gpsimd", "t", qw_free[og - 2])
        S.dma("gpsimd", qw_st[og % 2][:], qwh[og], "gw%d" % (og % 2))
        qw_ld[og] = S.dma("gpsimd", sb_st[og % 2][:], sbh[og], "gw%d" % (og % 2))

    def o2r_issue(i):
        og, rem = divmod(i, 8)
        o_loc, th = divmod(rem, 2)
        c = og * 4 + o_loc
        if i == 0:
            S.w("sync", "sc0", out2_done[0])
            S.w("sync", "sc1", out2_done[1])
            S.w("sync", "t", pass2_t)
        if i >= 4:
            S.w("sync", "v", fin_used[i - 4])
        o2r_ld[i] = S.dma("sync", o2r_st[:, i % 4, :],
                          out2sp[:, c, th * 512:(th + 1) * 512], "sx%d" % (i % 4))

    qw_issue(0)
    qw_issue(1)
    for i in range(4):
        o2r_issue(i)
    sv = None
    S.w("sync", "t", pass2_t)
    zsm_s = S.dma("sync", zsm_sb, zsm, "sb2")
    for kk in range(32):
        normalize_emit(kk)
        S.w("tensor", "v", gat_norm[kk])
        for th in range(2):
            mm(th, pone_bf[:, kk // 8, :], xn1g[:, kk, th * 512:(th + 1) * 512],
               start=(kk == 0), stop=(kk == 31), mrows=32,
               incs=[("t", 1)] if (kk == 31 and th == 1) else ())
    cs_t = S.val["t"]
    S.w("vector", "t", cs_t)
    S.o("vector", lambda e: e.tensor_copy(colsum_bf[:, 0:512], ps[0:32, 0, 0:512]))
    r = S.o("vector", lambda e: e.tensor_copy(colsum_bf[:, 512:1024], ps[0:32, 1, 0:512]),
            [("v", 1)])
    guard([0, 1], ("v", r["v"]))
    S.w("vector", "sb2", zsm_s)
    r = S.o("vector", lambda e: e.tensor_scalar(
        out=negz_bf, in0=zsm_sb, scalar1=-1.0, scalar2=None, op0=ALU.mult), [("v", 1)])
    negz_v = r["v"]
    wsl_t = {}
    for og in range(8):
        S.w("vector", "gw%d" % (og % 2), qw_ld[og])
        for q in range(4):
            for p in range(8):
                kk = q * 8 + p
                wi = kk % 4
                S.o("vector", lambda e, q=q, p=p, og=og: e.tensor_scalar(
                    out=nib_t[:, 0, :], in0=qw_st[og % 2][:, q, :],
                    scalar1=int(4 * p), scalar2=int(15),
                    op0=ALU.logical_shift_right, op1=ALU.bitwise_and))
                gk = og * 32 + kk
                if gk >= 4:
                    S.w("vector", "t", wsl_t[gk - 4])
                r = S.o("vector", lambda e, q=q, og=og, wi=wi: e.tensor_tensor(
                    out=wsl[:, wi, :], in0=nib_t[:, 0, :],
                    in1=sb_st[og % 2][:, q, :], op=ALU.mult), [("v", 1)])
                S.w("tensor", "v", r["v"])
                for o_loc in range(4):
                    for th in range(2):
                        bank = o_loc * 2 + th
                        mm(bank, wsl[:, wi, o_loc * 128:(o_loc + 1) * 128],
                           xn1g[:, kk, th * 512:(th + 1) * 512],
                           start=(kk == 0), stop=False,
                           incs=[("t", 1)] if (o_loc == 3 and th == 1) else ())
                wsl_t[gk] = S.val["t"]
        if og == 0:
            S.w("tensor", "v", negz_v)
        for o_loc in range(4):
            col0 = og * 512 + o_loc * 128
            for th in range(2):
                bank = o_loc * 2 + th
                mm(bank, negz_bf[:, col0:col0 + 128],
                   colsum_bf[:, th * 512:(th + 1) * 512],
                   start=False, stop=True,
                   incs=[("t", 1)] if (o_loc == 3 and th == 1) else ())
        qw_free[og] = S.val["t"]
        if og + 2 < 8:
            qw_issue(og + 2)
        tv = S.val["t"]
        S.w("vector", "t", tv)
        S.w("vector", "sp", persist_s)
        for o_loc in range(4):
            c = og * 4 + o_loc
            for th in range(2):
                i = og * 8 + o_loc * 2 + th
                bank = o_loc * 2 + th
                fsl = i % 4
                S.w("vector", "sx%d" % (i % 4), o2r_ld[i])
                if i >= 4:
                    S.w("vector", "so%d" % (i % 4), fin_store[i - 4])
                r = S.o("vector", lambda e, b=bank, c=c, fsl=fsl, i=i: e.scalar_tensor_tensor(
                    out=fin_st[:, fsl, :], in0=ps[:, b, :], scalar=bias_s[:, c:c + 1],
                    in1=o2r_st[:, i % 4, :], op0=ALU.add, op1=ALU.add), [("v", 1)])
                bank_guard[bank] = ("v", r["v"])
                fin_used[i] = r["v"]
                S.w("sync", "v", r["v"])
                fin_store[i] = S.dma("sync", out[:, c, th * 512:(th + 1) * 512], fin_st[:, fsl, :],
                                     "so%d" % (i % 4))
                if i + 4 < 64:
                    o2r_issue(i + 4)
    S.wait_queue("sync", "sync")
    S.wait_queue("gpsimd", "gpsimd")
    return nc, S, ctxmgrs


def build(stage="FULL"):
    nc, S, ctxmgrs = build_program(stage)
    semobjs = {}
    sem_cms = []
    for name in S.sem_names:
        cm = nc.semaphore(name)
        semobjs[name] = cm.__enter__()
        sem_cms.append(cm)
    with nc.Block() as block:
        S.emit(block, semobjs)
    for cm in reversed(sem_cms):
        cm.__exit__(None, None, None)
    for cm in reversed(ctxmgrs):
        cm.__exit__(None, None, None)
    return nc


def kernel(**inputs):
    d, shards, perm = host_prep(inputs)
    nc = build("FULL")
    in_maps = []
    for c in range(NCORE):
        m = dict(d)
        m["xt"] = shards[c]
        in_maps.append(m)
    res = run_bass_kernel_spmd(nc, in_maps, core_ids=list(range(NCORE)))
    outp = np.empty((1, 8192, 4096), np.float32)
    for c in range(NCORE):
        sh = res.results[c]["out"]
        slotmaj = sh.transpose(1, 0, 2).reshape(4096, T)
        seg = np.empty((T, 4096), np.float32)
        seg[:, perm] = slotmaj.T
        outp[0, c * T:(c + 1) * T, :] = seg
    return outp


# version 07
# speedup vs baseline: 1.0939x; 1.0235x over previous
import numpy as np
import ml_dtypes
import concourse.bass as bass
import concourse.mybir as mybir
from concourse.bass_utils import run_bass_kernel_spmd

F32 = mybir.dt.float32
BF16 = mybir.dt.bfloat16
I32 = mybir.dt.int32
U32 = mybir.dt.uint32
AF = mybir.ActivationFunctionType
ALU = mybir.AluOpType

ENGS = ("sync", "gpsimd", "tensor", "vector", "scalar")
T = 1024
NCT = 32
NCORE = 8
EPS_LN = 1e-5
EPS_DEN = 1e-6
DTSIZE = {F32: 4, BF16: 2, I32: 4, U32: 4}


class Sched:
    def __init__(self):
        self.prog = {e: [] for e in ENGS}
        self.val = {}
        self.sem_names = []
        self.dma_sems = set()
        self.queue_sems = {"sync": set(), "gpsimd": set()}

    def sem(self, name, dma=False):
        if name not in self.val:
            self.val[name] = 0
            self.sem_names.append(name)
            if dma:
                self.dma_sems.add(name)
        return name

    def w(self, eng, sname, v):
        if v <= 0:
            return
        assert v <= self.val[sname], f"wait {sname}>={v} but scheduled {self.val[sname]}"
        if sname in self.dma_sems:
            assert v == self.val[sname], \
                f"stale wait on dma sem {sname}: {v} < {self.val[sname]}"
        self.prog[eng].append(("w", sname, v))

    def o(self, eng, fn, incs=()):
        own = {"vector": "v", "scalar": "a"}.get(eng)
        if own is not None:
            if self.val[own] > 0:
                self.prog[eng].append(("w", own, self.val[own]))
            if not any(s == own for s, _ in incs):
                incs = tuple(incs) + ((own, 1),)
        for sname, n in incs:
            self.sem(sname)
            self.val[sname] += n
        self.prog[eng].append(("o", fn, tuple(incs)))
        return {sname: self.val[sname] for sname, n in incs}

    def dma(self, q, out, in_, sname):
        self.sem(sname, dma=True)
        self.queue_sems[q].add(sname)
        r = self.o(q, lambda e, o=out, i=in_: e.dma_start(out=o, in_=i), [(sname, 16)])
        return r[sname]

    def wait_queue(self, eng, q):
        for sname in sorted(self.queue_sems[q]):
            self.w(eng, sname, self.val[sname])

    def barrier(self):
        vals = {}
        for e in ENGS:
            sname = self.sem("bar_" + e)
            if e in ("sync", "gpsimd"):
                self.wait_queue(e, e)
                self.o(e, lambda en: en.nop(), [(sname, 1)])
            else:
                own = {"vector": "v", "scalar": "a", "tensor": "t"}[e]
                self.w(e, own, self.val[own])
                self.val[sname] += 1
                self.prog[e].append(("o", lambda en: en.nop(), ((sname, 1),)))
            vals[e] = self.val[sname]
        for e in ENGS:
            for f in ENGS:
                if f != e:
                    self.w(e, "bar_" + f, vals[f])

    def emit(self, block, semobjs):
        for ename in ENGS:
            prog = self.prog[ename]

            def body(eng, prog=prog):
                for item in prog:
                    if item[0] == "w":
                        eng.wait_ge(semobjs[item[1]], item[2])
                    else:
                        inst = item[1](eng)
                        incs = item[2]
                        if incs:
                            inst.then_inc(semobjs[incs[0][0]], incs[0][1])
                            for sname, n in incs[1:]:
                                eng.nop().then_inc(semobjs[sname], n)
            getattr(block, ename)(body)


def host_prep(inp):
    perm = np.asarray(inp["L0_ffn_perm"]).astype(np.int64)
    invperm = np.argsort(perm)
    d = {}
    f32 = np.float32

    Wq = np.asarray(inp["L0_attn_Wq"])
    sel = np.concatenate([np.arange(4 * j * 128, (4 * j + 1) * 128) for j in range(8)])
    A = Wq[sel][:, perm].T
    d["wq"] = np.ascontiguousarray(A.reshape(32, 128, 2, 512).transpose(2, 1, 0, 3)).astype(f32)
    d["bq2d"] = np.ascontiguousarray(np.asarray(inp["L0_attn_bq"])[sel].reshape(8, 128).T).astype(f32)
    d["m0h"] = np.ascontiguousarray(np.asarray(inp["L0_attn_M"]).transpose(1, 0, 2)).astype(f32)
    d["z0t"] = np.ascontiguousarray(np.asarray(inp["L0_attn_z"]).T).astype(f32)
    Wo = np.asarray(inp["L0_attn_Wo"])
    d["woh"] = np.ascontiguousarray(Wo[perm, :].T.reshape(32, 128, 4096)).astype(f32)
    up0 = np.asarray(inp["L0_ffn_up"])
    d["up0h"] = np.ascontiguousarray(up0.reshape(16, 2, 128, 1024).transpose(0, 2, 1, 3)).astype(f32)
    dn0 = np.asarray(inp["L0_ffn_down"])
    d["dn0h"] = np.ascontiguousarray(dn0.reshape(16, 8, 128, 256).transpose(0, 2, 1, 3)).astype(f32)
    B = np.asarray(inp["L1_attn_up_w"])[:, perm].T
    d["upw1"] = np.ascontiguousarray(B.reshape(32, 128, 2, 512).transpose(2, 1, 0, 3)).astype(f32)
    Bg = np.asarray(inp["L1_attn_gate_w"])[:, perm].T
    d["gtw1"] = np.ascontiguousarray(Bg.reshape(32, 128, 2, 512).transpose(2, 1, 0, 3)).astype(f32)
    d["lp2d"] = np.ascontiguousarray(np.asarray(inp["L1_attn_local_point"]).reshape(8, 128).T).astype(f32)
    L = np.asarray(inp["L1_attn_local_approx_output"])[perm, :].T
    d["laoh"] = np.ascontiguousarray(L.reshape(8, 128, 8, 512).transpose(2, 1, 0, 3)).astype(f32)
    fu = np.asarray(inp["L1_attn_fuse_weight"])[perm].transpose(2, 1, 0)
    d["fuseh"] = np.ascontiguousarray(fu.reshape(6, 8, 128, 8, 512).transpose(0, 3, 2, 1, 4)).astype(f32)
    disc = np.zeros((1, 8), f32)
    disc[0, :6] = np.asarray(inp["L1_attn_discount_factor"])
    d["disc"] = disc
    G = np.asarray(inp["L1_attn_ffn_gate_w"])[:, perm].T
    d["wg1"] = np.ascontiguousarray(G.reshape(32, 128, 39, 256).transpose(2, 1, 0, 3)).astype(f32)
    U = np.asarray(inp["L1_attn_ffn_up_w"])[:, perm].T
    d["wu1"] = np.ascontiguousarray(U.reshape(32, 128, 39, 256).transpose(2, 1, 0, 3)).astype(f32)
    D = np.asarray(inp["L1_attn_ffn_down_w"])[perm, :].T
    d["wd1"] = np.ascontiguousarray(D.reshape(78, 128, 8, 512).transpose(2, 1, 0, 3)).astype(f32)
    qw = np.asarray(inp["L1_ffn_q_weights"])[:, perm]
    d["qwh"] = np.ascontiguousarray(qw.reshape(4, 128, 8, 512).transpose(2, 1, 0, 3)).astype(np.int32)

    def repl(a):
        o = a.reshape(4, 8, 4096)
        o = np.broadcast_to(o[:, :, None, :], (4, 8, 16, 4096)).reshape(4, 128, 4096)
        return np.ascontiguousarray(o.reshape(4, 128, 8, 512).transpose(2, 1, 0, 3))
    d["sbh"] = repl(np.asarray(inp["L1_ffn_scales"])[:, perm]).astype(f32)
    d["zsm"] = np.ascontiguousarray(np.asarray(inp["L1_ffn_zeros"])[:, perm]).astype(f32)
    po = np.zeros((128, 4, 32), f32)
    rr = np.arange(128)
    for q in range(4):
        po[rr, q, 8 * q + rr // 16] = 1.0
    d["poneh"] = po
    d["bias2d"] = np.ascontiguousarray(np.asarray(inp["L1_ffn_bias"])[perm].reshape(32, 128).T).astype(f32)
    gi = np.empty((128, 32), np.uint32)
    r = np.arange(128)
    for q in range(4):
        for p in range(8):
            gi[:, q * 8 + p] = invperm[8 * (128 * q + r) + p]
    d["gidx"] = gi
    x = np.asarray(inp["L0_attn_hidden_states"])[0]
    xTp = np.ascontiguousarray(x[:, perm].T)
    shards = []
    for c in range(NCORE):
        sh = xTp[:, c * T:(c + 1) * T]
        shards.append(np.ascontiguousarray(sh.reshape(32, 128, T).transpose(1, 0, 2)).astype(f32))
    return d, shards, perm


def build_program(stage="FULL"):
    nc = bass.Bass("TRN2", target_bir_lowering=False)
    S = Sched()

    def din(name, shape, dtype):
        return nc.dram_tensor(name, list(shape), dtype, kind="ExternalInput").ap()

    xt = din("xt", (128, 32, T), F32)
    wq = din("wq", (2, 128, 32, 512), F32)
    bq2d = din("bq2d", (128, 8), F32)
    m0h = din("m0h", (128, 8, 128), F32)
    z0t = din("z0t", (128, 8), F32)
    woh = din("woh", (32, 128, 4096), F32)
    up0h = din("up0h", (16, 128, 2, 1024), F32)
    dn0h = din("dn0h", (16, 128, 8, 256), F32)
    upw1 = din("upw1", (2, 128, 32, 512), F32)
    gtw1 = din("gtw1", (2, 128, 32, 512), F32)
    lp2d = din("lp2d", (128, 8), F32)
    laoh = din("laoh", (8, 128, 8, 512), F32)
    fuseh = din("fuseh", (6, 8, 128, 8, 512), F32)
    disc = din("disc", (1, 8), F32)
    wg1 = din("wg1", (39, 128, 32, 256), F32)
    wu1 = din("wu1", (39, 128, 32, 256), F32)
    wd1 = din("wd1", (8, 128, 78, 512), F32)
    qwh = din("qwh", (8, 128, 4, 512), I32)
    sbh = din("sbh", (8, 128, 4, 512), F32)
    zsm = din("zsm", (32, 4096), F32)
    poneh = din("poneh", (128, 4, 32), F32)
    bias2d = din("bias2d", (128, 32), F32)
    gidx = din("gidx", (128, 32), U32)

    x1sp = nc.dram_tensor("x1sp", [4096, T], F32, kind="Internal").ap()
    out2sp = nc.dram_tensor("out2sp", [128, 32, T], F32, kind="Internal").ap()
    gusp = nc.dram_tensor("gusp", [128, 78, T], BF16, kind="Internal").ap()
    bcst = nc.dram_tensor("bcst", [1, 4096], F32, kind="Internal").ap()
    out = nc.dram_tensor("out", [128, 32, T], F32, kind="ExternalOutput").ap()

    ctxmgrs = []

    def sbuf(name, shape, dtype):
        cm = nc.sbuf_tensor(name, list(shape), dtype)
        t = cm.__enter__()
        ctxmgrs.append(cm)
        return t

    A1 = sbuf("A1", [128, 16384], F32)
    A2 = sbuf("A2", [128, 16384], F32)
    A3 = sbuf("A3", [128, 16384], F32)

    def cv(arena, off_bytes, n, width, dtype=BF16):
        nb = n * width * DTSIZE[dtype]
        assert off_bytes % 4 == 0 and off_bytes + nb <= 65536, (off_bytes, nb)
        v = arena[:, off_bytes // 4: (off_bytes + nb) // 4]
        if dtype != F32:
            v = v.bitcast(dtype)
        return v.rearrange("p (a b) -> p a b", b=width)

    def rowv(arena, off_bytes, width):
        return arena[0:1, off_bytes // 4: off_bytes // 4 + width]

    ones_bf = sbuf("ones_bf", [128, 1], BF16)
    mean_b = sbuf("mean_b", [128, T], F32)
    istd_b = sbuf("istd_b", [128, T], F32)
    bq_s = sbuf("bq_s", [128, 8], F32)
    lp_s = sbuf("lp_s", [128, 8], F32)
    z0_bf = sbuf("z0_bf", [128, 8], BF16)
    m0_bf = sbuf("m0_bf", [128, 8, 128], BF16)
    bias_s = sbuf("bias_s", [128, 32], F32)
    cvec = sbuf("cvec", [128, 8], F32)
    disc_s = sbuf("disc_s", [1, 8], F32)
    gidx_s = sbuf("gidx_s", [128, 32], U32)
    eps_c = sbuf("eps_c", [1, 1], F32)
    cvrow = sbuf("cvrow", [1, 8], F32)
    mean_bf = sbuf("mean_bf", [128, T], BF16)
    istd_bf = sbuf("istd_bf", [128, T], BF16)
    pone_bf = sbuf("pone_bf", [128, 4, 32], BF16)

    ps_cm = nc.psum_tensor("ps", [128, 8, 512], F32)
    ps = ps_cm.__enter__()
    ctxmgrs.append(ps_cm)

    for s_ in ("t", "v", "a", "ms"):
        S.sem(s_)
    bank_guard = [None] * 8

    def mm(bank, lhsT, rhs, start, stop, incs=(), mrows=None):
        if start and bank_guard[bank] is not None:
            S.w("tensor", bank_guard[bank][0], bank_guard[bank][1])
            bank_guard[bank] = None
        width = rhs.shape[-1]
        dst = ps[:, bank, 0:width] if mrows is None else ps[0:mrows, bank, 0:width]
        S.o("tensor", lambda e, d=dst, l=lhsT, r=rhs, st=start, sp=stop:
            e.matmul(d, l, r, start=st, stop=sp), incs)

    def guard(banks, ev):
        for b in banks:
            bank_guard[b] = ev

    S.o("gpsimd", lambda e: e.memset(ones_bf[:], 1.0), [("ms", 1)])
    S.o("gpsimd", lambda e: e.memset(eps_c[:], float(EPS_LN)), [("ms", 1)])
    S.dma("sync", bq_s[:], bq2d, "sp")
    S.dma("sync", lp_s[:], lp2d, "sp")
    S.dma("gpsimd", z0_bf[:], z0t, "gp")
    S.dma("gpsimd", m0_bf[:], m0h, "gp")
    S.dma("sync", bias_s[:], bias2d, "sp")
    S.dma("sync", disc_s[:], disc, "sp")
    persist_s = S.dma("sync", gidx_s[:], gidx, "sp")
    persist_g = S.dma("gpsimd", pone_bf[:], poneh, "gp")

    def stats_and_norm(src_bf, dst_bf, ready):
        sq = cv(A3, 36864, 2, T, BF16)
        rows_m = rowv(A3, 49152, 1024)
        rows_e = rowv(A3, 53248, 1024)
        rows_t = rowv(A3, 57344, 1024)
        rows_i = rowv(A3, 61440, 1024)
        t_marks = []
        S.w("tensor", "ms", 1)
        for c in range(NCT):
            if ready[c] is not None:
                S.w("scalar", ready[c][0], ready[c][1])
            if c >= 2:
                S.w("scalar", "t", t_marks[c - 2])
            S.o("scalar", lambda e, c=c: e.activation(sq[:, c % 2, :], src_bf[:, c, :], AF.Square),
                [("a", 1)])
            aval = S.val["a"]
            if ready[c] is not None:
                S.w("tensor", ready[c][0], ready[c][1])
            for th in range(2):
                mm(th, ones_bf[:], src_bf[:, c, th * 512:(th + 1) * 512],
                   start=(c == 0), stop=(c == NCT - 1), mrows=1)
            S.w("tensor", "a", aval)
            for th in range(2):
                mm(2 + th, ones_bf[:], sq[:, c % 2, th * 512:(th + 1) * 512],
                   start=(c == 0), stop=(c == NCT - 1), mrows=1,
                   incs=[("t", 1)] if th == 1 else ())
            t_marks.append(S.val["t"])
        S.w("vector", "t", t_marks[-1])
        for th in range(2):
            S.o("vector", lambda e, th=th: e.tensor_scalar(
                out=rows_m[:, th * 512:(th + 1) * 512], in0=ps[0:1, th, 0:512],
                scalar1=1.0 / 4096.0, scalar2=None, op0=ALU.mult))
            S.o("vector", lambda e, th=th: e.tensor_scalar(
                out=rows_e[:, th * 512:(th + 1) * 512], in0=ps[0:1, 2 + th, 0:512],
                scalar1=1.0 / 4096.0, scalar2=None, op0=ALU.mult))
        S.o("vector", lambda e: e.tensor_tensor(out=rows_t[:], in0=rows_m[:], in1=rows_m[:], op=ALU.mult))
        r = S.o("vector", lambda e: e.tensor_tensor(out=rows_e[:], in0=rows_e[:], in1=rows_t[:],
                                                    op=ALU.subtract), [("v", 1)])
        guard([0, 1, 2, 3], ("v", r["v"]))
        S.w("scalar", "v", r["v"])
        S.w("scalar", "ms", 2)
        r = S.o("scalar", lambda e: e.activation(rows_t[:], rows_e[:], AF.Sqrt, bias=eps_c[0:1, 0:1]),
                [("a", 1)])
        S.w("vector", "a", r["a"])
        r = S.o("vector", lambda e: e.reciprocal(rows_i[:], rows_t[:]), [("v", 1)])
        S.w("sync", "v", r["v"])
        s0 = S.dma("sync", bcst[0:1, 0:1024], rows_m[:], "sb0")
        s1 = S.dma("sync", bcst[0:1, 1024:2048], rows_i[:], "sb1")
        S.w("sync", "sb0", s0)
        S.w("sync", "sb1", s1)
        s0 = S.dma("sync", mean_b[:], bcst[0:1, 0:1024].partition_broadcast(128), "sb0")
        s1 = S.dma("sync", istd_b[:], bcst[0:1, 1024:2048].partition_broadcast(128), "sb1")
        tmpf = cv(A3, 40960, 2, T, F32)
        S.w("vector", "sb0", s0)
        S.w("vector", "sb1", s1)
        marks = []
        for c in range(NCT):
            S.o("vector", lambda e, c=c: e.tensor_tensor(
                out=tmpf[:, c % 2, :], in0=src_bf[:, c, :], in1=mean_b[:], op=ALU.subtract))
            r = S.o("vector", lambda e, c=c: e.tensor_tensor(
                out=dst_bf[:, c, :], in0=tmpf[:, c % 2, :], in1=istd_b[:], op=ALU.mult), [("v", 1)])
            marks.append(r["v"])
        return marks

    xbf = cv(A2, 0, 32, T, BF16)
    xn = xbf
    x_ready = []
    for ch in range(4):
        v = S.dma("gpsimd", xbf[:, ch * 8:(ch + 1) * 8, :], xt[:, ch * 8:(ch + 1) * 8, :],
                  "gx%d" % ch)
        x_ready += [("gx%d" % ch, v)] * 8
    WoS = cv(A3, 0, 8, 4096, BF16)

    def wos_emit(jlist, gate_v=0):
        first = True
        for gg in range(4):
            for j in jlist:
                sname = "gwo%d" % (j % 2)
                S.sem(sname, dma=True)
                S.queue_sems["gpsimd"].add(sname)
                if first and gate_v > 0:
                    S.w("gpsimd", "v", gate_v)
                first = False
                if gg > 0:
                    S.w("gpsimd", sname, S.val[sname])
                if gg == 0:
                    wos_emit.marks[j] = S.dma("gpsimd", WoS[:, j, :], woh[4 * j + gg], sname)
                else:
                    r = S.o("gpsimd", lambda e, j=j, gg=gg: e.dma_start(
                        out=WoS[:, j, :], in_=woh[4 * j + gg], accum_op=ALU.add,
                        max_dma_last_dim=4096),
                        [(sname, 16)])
                    wos_emit.marks[j] = r[sname]
    wos_emit.marks = {}
    wq_st = [cv(A1, 0, 8, 512, BF16), cv(A1, 8192, 8, 512, BF16)]
    wq_ld = {}
    wq_free = {}

    def wq_issue(idx):
        qg, cg = divmod(idx, 4)
        if idx >= 2:
            S.w("gpsimd", "t", wq_free[idx - 2])
        wq_ld[idx] = S.dma("gpsimd", wq_st[idx % 2][:], wq[qg, :, cg * 8:(cg + 1) * 8, :],
                           "gq%d" % (idx % 2))
        if idx == 7:
            wos_emit([4, 5, 6, 7], gate_v=xn_done)

    wq_issue(0)
    wq_issue(1)
    wos_emit([0, 1, 2, 3])
    xn_marks = stats_and_norm(xbf, xn, x_ready)
    xn_done = xn_marks[-1]

    qsel = cv(A1, 32768, 8, T, BF16)
    retr = cv(A1, 49152, 8, T, BF16)
    elu_t = cv(A1, 16384, 6, 512, F32)
    for qg in range(2):
        for c in range(NCT):
            cg = c // 8
            idx = qg * 4 + cg
            if c % 8 == 0:
                S.w("tensor", "gq%d" % (idx % 2), wq_ld[idx])
            if qg == 0:
                S.w("tensor", "v", xn_marks[c])
            for o_loc in range(4):
                for th in range(2):
                    bank = o_loc * 2 + th
                    mm(bank, wq_st[idx % 2][:, c % 8, o_loc * 128:(o_loc + 1) * 128],
                       xn[:, c, th * 512:(th + 1) * 512],
                       start=(c == 0), stop=(c == NCT - 1),
                       incs=[("t", 1)] if (c % 8 == 7 and o_loc == 3 and th == 1) else ())
            if c % 8 == 7:
                wq_free[idx] = S.val["t"]
                if idx + 2 < 8:
                    wq_issue(idx + 2)
        tdone = S.val["t"]
        S.w("vector", "t", tdone)
        S.w("vector", "sp", persist_s)
        for o_loc in range(4):
            j = qg * 4 + o_loc
            for th in range(2):
                bank = o_loc * 2 + th
                sl = bank % 2
                S.o("vector", lambda e, b=bank, j=j, sl=sl: e.tensor_scalar(
                    out=elu_t[:, sl, :], in0=ps[:, b, :], scalar1=bq_s[:, j:j + 1],
                    scalar2=0.0, op0=ALU.add, op1=ALU.min), [("v", 1)])
                vv = S.val["v"]
                S.w("scalar", "v", vv)
                r = S.o("scalar", lambda e, sl=sl: e.activation(
                    elu_t[:, 2 + sl, :], elu_t[:, sl, :], AF.Exp), [("a", 1)])
                av = r["a"]
                S.o("vector", lambda e, b=bank, j=j, sl=sl: e.tensor_scalar(
                    out=elu_t[:, 4 + sl, :], in0=ps[:, b, :], scalar1=bq_s[:, j:j + 1],
                    scalar2=0.0, op0=ALU.add, op1=ALU.max))
                S.w("vector", "a", av)
                r = S.o("vector", lambda e, j=j, th=th, sl=sl: e.tensor_tensor(
                    out=qsel[:, j, th * 512:(th + 1) * 512], in0=elu_t[:, 2 + sl, :],
                    in1=elu_t[:, 4 + sl, :], op=ALU.add), [("v", 1)])
                bank_guard[bank] = ("v", r["v"])
    qsel_done = S.val["v"]

    invden = [cv(A1, 8192, 1, T, F32), cv(A1, 12288, 1, T, F32)]
    rows_d = [rowv(A1, 16384, 1024), rowv(A1, 20480, 1024)]
    rows_r = [rowv(A1, 0, 1024), rowv(A1, 4096, 1024)]
    S.w("tensor", "gp", persist_g)
    S.w("tensor", "v", qsel_done)
    mm_t = {}

    def retr_mm(j):
        par = j % 2
        nb = [0, 1] if par == 0 else [2, 3]
        db = [4, 5] if par == 0 else [6, 7]
        for th in range(2):
            mm(nb[th], m0_bf[:, j, :], qsel[:, j, th * 512:(th + 1) * 512], start=True, stop=True)
        for th in range(2):
            mm(db[th], z0_bf[:, j:j + 1], qsel[:, j, th * 512:(th + 1) * 512],
               start=True, stop=True, mrows=1,
               incs=[("t", 1)] if th == 1 else ())
        mm_t[j] = S.val["t"]

    retr_mm(0)
    retr_mm(1)
    for j in range(8):
        par = j % 2
        nb = [0, 1] if par == 0 else [2, 3]
        db = [4, 5] if par == 0 else [6, 7]
        S.w("vector", "t", mm_t[j])
        for th in range(2):
            S.o("vector", lambda e, b=db[th], th=th, par=par: e.tensor_scalar(
                out=rows_d[par][:, th * 512:(th + 1) * 512], in0=ps[0:1, b, 0:512],
                scalar1=float(EPS_DEN), scalar2=None, op0=ALU.add))
        r = S.o("vector", lambda e, par=par: e.reciprocal(rows_r[par][:], rows_d[par][:]),
                [("v", 1)])
        guard(db, ("v", r["v"]))
        S.w("sync", "v", r["v"])
        sv = S.dma("sync", bcst[0:1, 2048 + 1024 * par:3072 + 1024 * par], rows_r[par][:],
                   "sb%d" % par)
        S.w("sync", "sb%d" % par, sv)
        sv = S.dma("sync", invden[par][:, 0, :],
                   bcst[0:1, 2048 + 1024 * par:3072 + 1024 * par].partition_broadcast(128),
                   "sb%d" % par)
        S.w("vector", "sb%d" % par, sv)
        for th in range(2):
            r = S.o("vector", lambda e, b=nb[th], j=j, th=th, par=par: e.tensor_tensor(
                out=retr[:, j, th * 512:(th + 1) * 512], in0=ps[:, b, :],
                in1=invden[par][:, 0, th * 512:(th + 1) * 512], op=ALU.mult),
                [("v", 1)])
            bank_guard[nb[th]] = ("v", r["v"])
        if j + 2 < 8:
            retr_mm(j + 2)
    retr_done = S.val["v"]

    up_st = [cv(A1, 0, 2, T, BF16), cv(A1, 4096, 2, T, BF16)]
    dn_st = [cv(A1, 8192, 8, 256, BF16), cv(A1, 12288, 8, 256, BF16)]
    x_st = cv(A1, 16384, 4, 512, F32)
    x1_st = cv(A1, 24576, 4, 512, F32)
    t1g = cv(A1, 32768, 8, T, BF16)
    S.w("gpsimd", "v", retr_done)
    S.w("sync", "v", retr_done)
    S.w("tensor", "gwo0", S.val["gwo0"])
    S.w("tensor", "gwo1", S.val["gwo1"])
    up_ld = {}
    up_free = {}
    x_ld = {}
    x_used = {}
    x1_store = {}

    def up_issue(h):
        if h >= 2:
            S.w("gpsimd", "t", up_free[h - 2])
        S.dma("gpsimd", up_st[h % 2][:], up0h[h], "gu%d" % (h % 2))
        up_ld[h] = S.dma("gpsimd", dn_st[h % 2][:], dn0h[h], "gu%d" % (h % 2))

    def x_issue(i):
        h, rem = divmod(i, 4)
        c_loc, th = divmod(rem, 2)
        if i >= 4:
            S.w("sync", "v", x_used[i - 4])
        x_ld[i] = S.dma("sync", x_st[:, i % 4, :],
                        xt[:, 2 * h + c_loc, th * 512:(th + 1) * 512], "sx%d" % (i % 4))

    up_issue(0)
    up_issue(1)
    for i in range(4):
        x_issue(i)
    for h in range(16):
        slot = h % 2
        S.w("tensor", "gu%d" % (h % 2), up_ld[h])
        gelu_th = {}
        for th in range(2):
            for mg in range(2):
                bankset = mg * 4
                for m_loc in range(4):
                    bank = bankset + m_loc
                    for k in range(2):
                        mm(bank, up_st[slot][:, k, mg * 512 + m_loc * 128: mg * 512 + (m_loc + 1) * 128],
                           xn[:, 2 * h + k, th * 512:(th + 1) * 512],
                           start=(k == 0), stop=(k == 1),
                           incs=[("t", 1)] if (m_loc == 3 and k == 1) else ())
                tv = S.val["t"]
                S.w("scalar", "t", tv)
                for m_loc in range(4):
                    bank = bankset + m_loc
                    r = S.o("scalar", lambda e, b=bank, mt=mg * 4 + m_loc, th=th: e.activation(
                        t1g[:, mt, th * 512:(th + 1) * 512], ps[:, b, :], AF.Gelu),
                        [("a", 1)])
                    bank_guard[bank] = ("a", r["a"])
            gelu_th[th] = S.val["a"]
        for c_loc in range(2):
            c = 2 * h + c_loc
            for th in range(2):
                i = h * 4 + c_loc * 2 + th
                bank = c_loc * 2 + th
                S.w("tensor", "a", gelu_th[th])
                for mk in range(8):
                    mm(bank, dn_st[slot][:, mk, c_loc * 128:(c_loc + 1) * 128],
                       t1g[:, mk, th * 512:(th + 1) * 512],
                       start=(mk == 0), stop=False)
                for j in range(8):
                    mm(bank, WoS[:, j, c * 128:(c + 1) * 128],
                       retr[:, j, th * 512:(th + 1) * 512],
                       start=False, stop=(j == 7),
                       incs=[("t", 1)] if j == 7 else ())
                tv = S.val["t"]
                S.w("vector", "t", tv)
                S.w("vector", "sx%d" % (i % 4), x_ld[i])
                if i >= 4:
                    S.w("vector", "so%d" % (i % 4), x1_store[i - 4])
                r = S.o("vector", lambda e, b=bank, i=i: e.tensor_tensor(
                    out=x1_st[:, i % 4, :], in0=ps[:, b, :], in1=x_st[:, i % 4, :], op=ALU.add),
                    [("v", 1)])
                bank_guard[bank] = ("v", r["v"])
                x_used[i] = r["v"]
                S.w("scalar", "v", r["v"])
                S.o("scalar", lambda e, c=c, th=th, i=i: e.copy(
                    xn[:, c, th * 512:(th + 1) * 512], x1_st[:, i % 4, :]), [("a", 1)])
                S.w("sync", "v", r["v"])
                sv = S.dma("sync", x1sp[128 * c:128 * (c + 1), th * 512:(th + 1) * 512],
                           x1_st[:, i % 4, :], "so%d" % (i % 4))
                if stage == "x1":
                    sv = S.dma("sync", out[:, c, th * 512:(th + 1) * 512], x1_st[:, i % 4, :],
                               "so%d" % (i % 4))
                x1_store[i] = sv
                if i + 4 < 64:
                    x_issue(i + 4)
        up_free[h] = S.val["t"]
        if h + 2 < 16:
            up_issue(h + 2)

    if stage == "x1":
        S.wait_queue("sync", "sync")
        S.wait_queue("gpsimd", "gpsimd")
        return nc, S, ctxmgrs

    S.barrier()
    for b in range(8):
        bank_guard[b] = None
    x1bf = xn
    xn1 = cv(A1, 0, 32, T, BF16)
    xn1_marks = stats_and_norm(x1bf, xn1, [None] * NCT)
    xn1_done = xn1_marks[-1]

    S.w("scalar", "sp", persist_s)
    r = S.o("scalar", lambda e: e.activation(cvrow[:], disc_s[:], AF.Exp, scale=-1.0),
            [("a", 1)])
    S.w("sync", "a", r["a"])
    sv = S.dma("sync", bcst[0:1, 3072:3080], cvrow[:], "sb2")
    S.w("sync", "sb2", sv)
    cvec_s = S.dma("sync", cvec[:], bcst[0:1, 3072:3080].partition_broadcast(128), "sb2")

    au = cv(A2, 0, 8, T, BF16)
    keep = cv(A2, 16384, 8, T, BF16)
    delta = cv(A3, 0, 8, T, BF16)
    w_st = [cv(A3, 16384, 8, 512, BF16), cv(A3, 24576, 8, 512, BF16)]
    ld_marks = {}
    ld_free = {}
    srcs = [(upw1, 0), (upw1, 1), (gtw1, 0), (gtw1, 1)]

    def iii_issue(i):
        blk, cg = divmod(i, 4)
        wt, sg = srcs[blk]
        if i >= 2:
            S.w("gpsimd", "t", ld_free[i - 2])
        ld_marks[i] = S.dma("gpsimd", w_st[i % 2][:], wt[sg, :, cg * 8:(cg + 1) * 8, :],
                            "gw%d" % (i % 2))

    iii_issue(0)
    iii_issue(1)
    for blk in range(4):
        which = "au" if blk < 2 else "delta"
        sg = blk % 2
        for c in range(NCT):
            i = blk * 4 + c // 8
            if c % 8 == 0:
                S.w("tensor", "gw%d" % (i % 2), ld_marks[i])
            if blk == 0:
                S.w("tensor", "v", xn1_marks[c])
            for s_loc in range(4):
                for th in range(2):
                    bank = s_loc * 2 + th
                    mm(bank, w_st[i % 2][:, c % 8, s_loc * 128:(s_loc + 1) * 128],
                       xn1[:, c, th * 512:(th + 1) * 512],
                       start=(c == 0), stop=(c == NCT - 1),
                       incs=[("t", 1)] if (c % 8 == 7 and s_loc == 3 and th == 1) else ())
            if c % 8 == 7:
                ld_free[i] = S.val["t"]
                if i + 2 < 16:
                    iii_issue(i + 2)
        tdone = S.val["t"]
        if which == "au":
            S.w("scalar", "t", tdone)
            for s_loc in range(4):
                sk = sg * 4 + s_loc
                for th in range(2):
                    bank = s_loc * 2 + th
                    r = S.o("scalar", lambda e, b=bank, sk=sk, th=th: e.copy(
                        au[:, sk, th * 512:(th + 1) * 512], ps[:, b, :]), [("a", 1)])
                    bank_guard[bank] = ("a", r["a"])
        else:
            S.w("vector", "t", tdone)
            S.w("vector", "sp", persist_s)
            for s_loc in range(4):
                sk = sg * 4 + s_loc
                for th in range(2):
                    bank = s_loc * 2 + th
                    r = S.o("vector", lambda e, b=bank, sk=sk, th=th: e.tensor_scalar(
                        out=delta[:, sk, th * 512:(th + 1) * 512], in0=ps[:, b, :],
                        scalar1=lp_s[:, sk:sk + 1], scalar2=None, op0=ALU.subtract),
                        [("v", 1)])
                    bank_guard[bank] = ("v", r["v"])
    au_done_a = S.val["a"]
    delta_done = S.val["v"]

    wg_st = [cv(A2, 32768, 32, 256, BF16), cv(A3, 32768, 32, 256, BF16)]
    wu_st = [cv(A2, 49152, 32, 256, BF16), cv(A3, 49152, 32, 256, BF16)]
    silu_t = cv(A3, 16384, 4, 512, F32)
    gu_st = [cv(A3, 24576, 2, T, BF16), cv(A3, 28672, 2, T, BF16)]
    wg_ld = {}
    wg_free = {}
    gu_sp = {}

    def v_issue(fg):
        if fg == 0:
            S.w("gpsimd", "v", delta_done)
            S.w("gpsimd", "a", au_done_a)
        if fg >= 2:
            S.w("gpsimd", "t", wg_free[fg - 2])
        S.dma("gpsimd", wg_st[fg % 2][:], wg1[fg], "gw%d" % (fg % 2))
        wg_ld[fg] = S.dma("gpsimd", wu_st[fg % 2][:], wu1[fg], "gw%d" % (fg % 2))

    v_issue(0)
    v_issue(1)
    for fg in range(39):
        slot = fg % 2
        S.w("tensor", "gw%d" % (fg % 2), wg_ld[fg])
        for c in range(NCT):
            for floc in range(2):
                for th in range(2):
                    mm(floc * 2 + th, wg_st[slot][:, c, floc * 128:(floc + 1) * 128],
                       xn1[:, c, th * 512:(th + 1) * 512], start=(c == 0), stop=(c == NCT - 1))
                    mm(4 + floc * 2 + th, wu_st[slot][:, c, floc * 128:(floc + 1) * 128],
                       xn1[:, c, th * 512:(th + 1) * 512], start=(c == 0), stop=(c == NCT - 1),
                       incs=[("t", 1)] if (c == NCT - 1 and floc == 1 and th == 1) else ())
        wg_free[fg] = S.val["t"]
        if fg + 2 < 39:
            v_issue(fg + 2)
        tv = S.val["t"]
        S.w("scalar", "t", tv)
        S.w("vector", "t", tv)
        if fg >= 2:
            S.w("vector", "sg%d" % (fg % 2), gu_sp[fg - 2])
        for floc in range(2):
            for th in range(2):
                gb = floc * 2 + th
                ub = 4 + floc * 2 + th
                ssl = floc * 2 + th
                r = S.o("scalar", lambda e, b=gb, ssl=ssl: e.activation(
                    silu_t[:, ssl, :], ps[:, b, :], AF.Silu), [("a", 1)])
                bank_guard[gb] = ("a", r["a"])
                S.w("vector", "a", r["a"])
                r = S.o("vector", lambda e, b=ub, ssl=ssl, gsl=slot, floc=floc, th=th: e.tensor_tensor(
                    out=gu_st[gsl][:, floc, th * 512:(th + 1) * 512], in0=silu_t[:, ssl, :],
                    in1=ps[:, b, :], op=ALU.mult), [("v", 1)])
                bank_guard[ub] = ("v", r["v"])
        vv = S.val["v"]
        S.w("sync", "v", vv)
        gu_sp[fg] = S.dma("sync", gusp[:, 2 * fg:2 * fg + 2, :], gu_st[slot][:],
                          "sg%d" % (fg % 2))
    gu_done = (S.val["sg0"], S.val["sg1"])
    ffn2_t = S.val["t"]

    pbuf = cv(A3, 16384, 8, T, BF16)
    opnd = [au,
            cv(A1, 0, 8, T, BF16), cv(A1, 16384, 8, T, BF16),
            cv(A1, 32768, 8, T, BF16), cv(A1, 49152, 8, T, BF16),
            cv(A3, 49152, 8, T, BF16), cv(A3, 0, 8, T, BF16)]
    opnd_mark = {}
    S.w("vector", "sb2", cvec_s)
    S.w("vector", "sg0", gu_done[0])
    S.w("vector", "sg1", gu_done[1])
    S.w("vector", "t", ffn2_t)
    S.o("vector", lambda e: e.tensor_scalar(out=opnd[1][:], in0=delta[:],
                                            scalar1=cvec[:, 0:1], scalar2=None, op0=ALU.mult))
    r = S.o("vector", lambda e: e.tensor_tensor(out=opnd[1][:], in0=opnd[1][:], in1=au[:], op=ALU.mult))
    opnd_mark[1] = r["v"]
    S.o("vector", lambda e: e.tensor_tensor(out=pbuf[:], in0=delta[:], in1=delta[:], op=ALU.mult))
    S.o("vector", lambda e: e.tensor_scalar(out=keep[:], in0=pbuf[:], scalar1=float(2.5 ** 2),
                                            scalar2=None, op0=ALU.is_le))
    S.o("vector", lambda e: e.tensor_scalar(out=opnd[2][:], in0=pbuf[:],
                                            scalar1=cvec[:, 1:2], scalar2=None, op0=ALU.mult))
    r = S.o("vector", lambda e: e.tensor_tensor(out=opnd[2][:], in0=opnd[2][:], in1=au[:], op=ALU.mult))
    opnd_mark[2] = r["v"]
    for m_ in (3, 4, 5, 6):
        S.o("vector", lambda e: e.tensor_tensor(out=pbuf[:], in0=pbuf[:], in1=delta[:], op=ALU.mult))
        if m_ >= 5:
            S.o("vector", lambda e, m_=m_: e.tensor_tensor(out=opnd[m_][:], in0=pbuf[:], in1=keep[:],
                                                           op=ALU.mult))
            S.o("vector", lambda e, m_=m_: e.tensor_scalar(out=opnd[m_][:], in0=opnd[m_][:],
                                                           scalar1=cvec[:, m_ - 1:m_], scalar2=None,
                                                           op0=ALU.mult))
        else:
            S.o("vector", lambda e, m_=m_: e.tensor_scalar(out=opnd[m_][:], in0=pbuf[:],
                                                           scalar1=cvec[:, m_ - 1:m_], scalar2=None,
                                                           op0=ALU.mult))
        r = S.o("vector", lambda e, m_=m_: e.tensor_tensor(out=opnd[m_][:], in0=opnd[m_][:], in1=au[:],
                                                           op=ALU.mult), [("v", 1)])
        opnd_mark[m_] = r["v"]
    opnd_done = S.val["v"]

    r = S.o("scalar", lambda e: e.copy(mean_bf[:], mean_b[:]))
    r = S.o("scalar", lambda e: e.copy(istd_bf[:], istd_b[:]), [("a", 1)])
    stats_bf_a = r["a"]

    CH = 5
    chunks = [(i * CH, min(CH, 78 - i * CH)) for i in range((78 + CH - 1) // CH)]
    lf_st = [cv(A3, 16384, 8, 512, BF16), cv(A3, 24576, 8, 512, BF16)]
    dn2_st = [cv(A2, 32768, CH, 512, BF16), cv(A2, 37888, CH, 512, BF16)]
    gu2_st = [cv(A2, 43008, CH, T, BF16), cv(A2, 53248, CH, T, BF16)]
    o2_st = cv(A3, 32768, 2, 512, F32)
    x1r_st = cv(A3, 36864, 2, 512, F32)
    gat_bf = cv(A2, 16384, 8, T, BF16)
    lf_ld = {}
    lf_free = {}
    dn2_ld = {}
    dn2_free = {}
    o2_marks = {}
    x1r_ld = {}
    x1r_free = {}
    nch = len(chunks)

    def lf_issue(n):
        og, ti = divmod(n, 7)
        src = laoh[og] if ti == 0 else fuseh[ti - 1, og]
        if n == 0:
            S.w("gpsimd", "v", opnd_done)
        if n >= 2:
            S.w("gpsimd", "t", lf_free[n - 2])
        lf_ld[n] = S.dma("gpsimd", lf_st[n % 2][:], src, "gl%d" % (n % 2))

    def dn2_issue(n):
        og, ci = divmod(n, nch)
        f0, cnt = chunks[ci]
        if n == 0:
            S.w("gpsimd", "t", ffn2_t)
            S.w("sync", "t", ffn2_t)
            S.w("sync", "sg0", gu_done[0])
            S.w("sync", "sg1", gu_done[1])
        if n >= 2:
            S.w("gpsimd", "t", dn2_free[n - 2])
            S.w("sync", "t", dn2_free[n - 2])
        gm = S.dma("gpsimd", dn2_st[n % 2][:, 0:cnt, :], wd1[og, :, f0:f0 + cnt, :],
                   "gd%d" % (n % 2))
        sm = S.dma("sync", gu2_st[n % 2][:, 0:cnt, :], gusp[:, f0:f0 + cnt, :],
                   "sr%d" % (n % 2))
        dn2_ld[n] = (gm, sm)

    def x1r_issue(i):
        og, rem = divmod(i, 8)
        o_loc, th = divmod(rem, 2)
        c = og * 4 + o_loc
        if i == 0:
            S.w("sync", "t", ffn2_t)
        if i >= 2:
            S.w("sync", "v", x1r_free[i - 2])
        x1r_ld[i] = S.dma("sync", x1r_st[:, i % 2, :],
                          x1sp[128 * c:128 * (c + 1), th * 512:(th + 1) * 512],
                          "sx%d" % (2 + i % 2))

    lf_issue(0)
    lf_issue(1)
    dn2_issue(0)
    dn2_issue(1)
    x1r_issue(0)
    x1r_issue(1)

    gat_ld = {}
    gat_norm = {}

    def gat_issue(kk):
        if kk == 0:
            S.w("gpsimd", "v", opnd_done)
            S.w("gpsimd", "sp", persist_s)
        if kk >= 8:
            S.w("gpsimd", "v", gat_norm[kk - 8])
        sname = S.sem("gg%d" % (kk % 8), dma=True)
        S.queue_sems["gpsimd"].add(sname)
        gat_ld[kk] = S.o("gpsimd", lambda e, kk=kk: e.indirect_dma_start(
            out=gat_bf[:, kk % 8, :], out_offset=None, in_=x1sp,
            in_offset=bass.IndirectOffsetOnAxis(ap=gidx_s[:, kk:kk + 1], axis=0)),
            [(sname, 16)])[sname]

    for og in range(8):
        if og == 1:
            for kk in range(8):
                gat_issue(kk)
        for ci in range(nch):
            n = og * nch + ci
            f0, cnt = chunks[ci]
            gm, sm = dn2_ld[n]
            S.w("tensor", "gd%d" % (n % 2), gm)
            S.w("tensor", "sr%d" % (n % 2), sm)
            for fi in range(cnt):
                for o_loc in range(4):
                    for th in range(2):
                        bank = o_loc * 2 + th
                        mm(bank, dn2_st[n % 2][:, fi, o_loc * 128:(o_loc + 1) * 128],
                           gu2_st[n % 2][:, fi, th * 512:(th + 1) * 512],
                           start=(ci == 0 and fi == 0), stop=False,
                           incs=[("t", 1)] if (fi == cnt - 1 and o_loc == 3 and th == 1) else ())
            dn2_free[n] = S.val["t"]
            if n + 2 < 8 * nch:
                dn2_issue(n + 2)
        for ti in range(7):
            n = og * 7 + ti
            S.w("tensor", "gl%d" % (n % 2), lf_ld[n])
            if ti >= 1:
                S.w("tensor", "v", opnd_mark[ti])
            for sk in range(8):
                for o_loc in range(4):
                    for th in range(2):
                        bank = o_loc * 2 + th
                        mm(bank, lf_st[n % 2][:, sk, o_loc * 128:(o_loc + 1) * 128],
                           opnd[ti][:, sk, th * 512:(th + 1) * 512],
                           start=False, stop=(ti == 6 and sk == 7),
                           incs=[("t", 1)] if (sk == 7 and o_loc == 3 and th == 1) else ())
            lf_free[n] = S.val["t"]
            if n + 2 < 56:
                lf_issue(n + 2)
        tv = S.val["t"]
        S.w("vector", "t", tv)
        for o_loc in range(4):
            c = og * 4 + o_loc
            for th in range(2):
                i = og * 8 + o_loc * 2 + th
                bank = o_loc * 2 + th
                S.w("vector", "sx%d" % (2 + i % 2), x1r_ld[i])
                if i >= 2:
                    S.w("vector", "sc%d" % (i % 2), o2_marks[i - 2])
                r = S.o("vector", lambda e, b=bank, osl=i % 2: e.tensor_tensor(
                    out=o2_st[:, osl, :], in0=ps[:, b, :], in1=x1r_st[:, osl, :],
                    op=ALU.add), [("v", 1)])
                bank_guard[bank] = ("v", r["v"])
                x1r_free[i] = r["v"]
                S.w("sync", "v", r["v"])
                o2_marks[i] = S.dma("sync", out2sp[:, c, th * 512:(th + 1) * 512],
                                    o2_st[:, i % 2, :], "sc%d" % (i % 2))
                if i + 2 < 64:
                    x1r_issue(i + 2)
    out2_done = (S.val["sc0"], S.val["sc1"])
    pass2_t = S.val["t"]

    qw_st = [cv(A2, 0, 4, 512, I32), cv(A2, 8192, 4, 512, I32)]
    sb_st = [cv(A2, 32768, 4, 512, F32), cv(A2, 40960, 4, 512, F32)]
    nib_t = cv(A3, 57344, 2, 512, I32)
    wsl = cv(A3, 61440, 4, 512, BF16)
    o2r_st = cv(A3, 40960, 4, 512, F32)
    fin_st = cv(A3, 49152, 4, 512, F32)
    xn1g = cv(A1, 0, 32, T, BF16)
    zsm_sb = A3[0:32, 0:4096]
    negz_bf = A3[0:32, 4096:6144].bitcast(BF16)
    colsum_bf = A3[0:32, 6144:6656].bitcast(BF16)
    qw_ld = {}
    qw_free = {}
    o2r_ld = {}
    fin_used = {}
    fin_store = {}

    def normalize_emit(kk):
        S.w("vector", "gg%d" % (kk % 8), gat_ld[kk])
        if kk == 0:
            S.w("vector", "t", pass2_t)
            S.w("vector", "a", stats_bf_a)
        S.o("vector", lambda e, kk=kk: e.tensor_tensor(
            out=xn1g[:, kk, :], in0=gat_bf[:, kk % 8, :], in1=mean_bf[:], op=ALU.subtract))
        r = S.o("vector", lambda e, kk=kk: e.tensor_tensor(
            out=xn1g[:, kk, :], in0=xn1g[:, kk, :], in1=istd_bf[:], op=ALU.mult), [("v", 1)])
        gat_norm[kk] = r["v"]
        if kk + 8 < 32:
            gat_issue(kk + 8)

    def qw_issue(og):
        if og == 0:
            S.w("gpsimd", "t", pass2_t)
        if og >= 2:
            S.w("gpsimd", "t", qw_free[og - 2])
        S.dma("gpsimd", qw_st[og % 2][:], qwh[og], "gw%d" % (og % 2))
        qw_ld[og] = S.dma("gpsimd", sb_st[og % 2][:], sbh[og], "gw%d" % (og % 2))

    def o2r_issue(i):
        og, rem = divmod(i, 8)
        o_loc, th = divmod(rem, 2)
        c = og * 4 + o_loc
        if i == 0:
            S.w("sync", "sc0", out2_done[0])
            S.w("sync", "sc1", out2_done[1])
            S.w("sync", "t", pass2_t)
        if i >= 4:
            S.w("sync", "v", fin_used[i - 4])
        o2r_ld[i] = S.dma("sync", o2r_st[:, i % 4, :],
                          out2sp[:, c, th * 512:(th + 1) * 512], "sx%d" % (i % 4))

    qw_issue(0)
    qw_issue(1)
    for i in range(4):
        o2r_issue(i)
    sv = None
    S.w("sync", "t", pass2_t)
    zsm_s = S.dma("sync", zsm_sb, zsm, "sb2")
    for kk in range(32):
        normalize_emit(kk)
        S.w("tensor", "v", gat_norm[kk])
        for th in range(2):
            mm(th, pone_bf[:, kk // 8, :], xn1g[:, kk, th * 512:(th + 1) * 512],
               start=(kk == 0), stop=(kk == 31), mrows=32,
               incs=[("t", 1)] if (kk == 31 and th == 1) else ())
    cs_t = S.val["t"]
    S.w("vector", "t", cs_t)
    S.o("vector", lambda e: e.tensor_copy(colsum_bf[:, 0:512], ps[0:32, 0, 0:512]))
    r = S.o("vector", lambda e: e.tensor_copy(colsum_bf[:, 512:1024], ps[0:32, 1, 0:512]),
            [("v", 1)])
    guard([0, 1], ("v", r["v"]))
    S.w("vector", "sb2", zsm_s)
    r = S.o("vector", lambda e: e.tensor_scalar(
        out=negz_bf, in0=zsm_sb, scalar1=-1.0, scalar2=None, op0=ALU.mult), [("v", 1)])
    negz_v = r["v"]
    wsl_t = {}
    for og in range(8):
        S.w("vector", "gw%d" % (og % 2), qw_ld[og])
        for q in range(4):
            for p in range(8):
                kk = q * 8 + p
                wi = kk % 4
                S.o("vector", lambda e, q=q, p=p, og=og: e.tensor_scalar(
                    out=nib_t[:, 0, :], in0=qw_st[og % 2][:, q, :],
                    scalar1=int(4 * p), scalar2=int(15),
                    op0=ALU.logical_shift_right, op1=ALU.bitwise_and))
                gk = og * 32 + kk
                if gk >= 4:
                    S.w("vector", "t", wsl_t[gk - 4])
                r = S.o("vector", lambda e, q=q, og=og, wi=wi: e.tensor_tensor(
                    out=wsl[:, wi, :], in0=nib_t[:, 0, :],
                    in1=sb_st[og % 2][:, q, :], op=ALU.mult), [("v", 1)])
                S.w("tensor", "v", r["v"])
                for o_loc in range(4):
                    for th in range(2):
                        bank = o_loc * 2 + th
                        mm(bank, wsl[:, wi, o_loc * 128:(o_loc + 1) * 128],
                           xn1g[:, kk, th * 512:(th + 1) * 512],
                           start=(kk == 0), stop=False,
                           incs=[("t", 1)] if (o_loc == 3 and th == 1) else ())
                wsl_t[gk] = S.val["t"]
        if og == 0:
            S.w("tensor", "v", negz_v)
        for o_loc in range(4):
            col0 = og * 512 + o_loc * 128
            for th in range(2):
                bank = o_loc * 2 + th
                mm(bank, negz_bf[:, col0:col0 + 128],
                   colsum_bf[:, th * 512:(th + 1) * 512],
                   start=False, stop=True,
                   incs=[("t", 1)] if (o_loc == 3 and th == 1) else ())
        qw_free[og] = S.val["t"]
        if og + 2 < 8:
            qw_issue(og + 2)
        tv = S.val["t"]
        S.w("vector", "t", tv)
        S.w("vector", "sp", persist_s)
        for o_loc in range(4):
            c = og * 4 + o_loc
            for th in range(2):
                i = og * 8 + o_loc * 2 + th
                bank = o_loc * 2 + th
                fsl = i % 4
                S.w("vector", "sx%d" % (i % 4), o2r_ld[i])
                if i >= 4:
                    S.w("vector", "so%d" % (i % 4), fin_store[i - 4])
                r = S.o("vector", lambda e, b=bank, c=c, fsl=fsl, i=i: e.scalar_tensor_tensor(
                    out=fin_st[:, fsl, :], in0=ps[:, b, :], scalar=bias_s[:, c:c + 1],
                    in1=o2r_st[:, i % 4, :], op0=ALU.add, op1=ALU.add), [("v", 1)])
                bank_guard[bank] = ("v", r["v"])
                fin_used[i] = r["v"]
                S.w("sync", "v", r["v"])
                fin_store[i] = S.dma("sync", out[:, c, th * 512:(th + 1) * 512], fin_st[:, fsl, :],
                                     "so%d" % (i % 4))
                if i + 4 < 64:
                    o2r_issue(i + 4)
    S.wait_queue("sync", "sync")
    S.wait_queue("gpsimd", "gpsimd")
    return nc, S, ctxmgrs


def build(stage="FULL"):
    nc, S, ctxmgrs = build_program(stage)
    semobjs = {}
    sem_cms = []
    for name in S.sem_names:
        cm = nc.semaphore(name)
        semobjs[name] = cm.__enter__()
        sem_cms.append(cm)
    with nc.Block() as block:
        S.emit(block, semobjs)
    for cm in reversed(sem_cms):
        cm.__exit__(None, None, None)
    for cm in reversed(ctxmgrs):
        cm.__exit__(None, None, None)
    return nc


def kernel(**inputs):
    d, shards, perm = host_prep(inputs)
    nc = build("FULL")
    in_maps = []
    for c in range(NCORE):
        m = dict(d)
        m["xt"] = shards[c]
        in_maps.append(m)
    res = run_bass_kernel_spmd(nc, in_maps, core_ids=list(range(NCORE)))
    outp = np.empty((1, 8192, 4096), np.float32)
    for c in range(NCORE):
        sh = res.results[c]["out"]
        slotmaj = sh.transpose(1, 0, 2).reshape(4096, T)
        seg = np.empty((T, 4096), np.float32)
        seg[:, perm] = slotmaj.T
        outp[0, c * T:(c + 1) * T, :] = seg
    return outp
